# version 06
# baseline (speedup 1.0000x reference)
import contextlib
import math
import os
os.environ.setdefault("GAUGE_NTFF_IGNORE_DMA", "1")
import numpy as np
import ml_dtypes
import concourse.bass as bass
import concourse.mybir as mybir
from concourse.bass_utils import run_bass_kernel_spmd

F32 = mybir.dt.float32
BF16 = mybir.dt.bfloat16
AF = mybir.ActivationFunctionType
ALU = mybir.AluOpType
AX = mybir.AxisListType
BF = ml_dtypes.bfloat16

D = 1024
SEQ = 4096
NCH = D // 128
NTT = SEQ // 128
EPS_LN = 1e-5
N_CORES = 8
DEV_SLICES = None
DEV_MAMBA_PHASE_A_ONLY = False
DEV_MB_DIRS = ("f", "b")
DEV_MB_DEBUG = False


class Buf:
    __slots__ = ("name", "writers", "readers")
    def __init__(self, name=""):
        self.name = name
        self.writers = {}
        self.readers = {}

class Prog:
    ENGS = ("tensor", "vector", "scalar", "gpsimd", "sync")
    DMAQ = ("sync", "gpsimd", "scalar", "vector")
    NDMASEM = 8

    def __init__(self, nc):
        self.nc = nc
        self.stream = {e: [] for e in self.ENGS}
        self.count = {}
        self.seen = {e: {} for e in self.ENGS}
        self.n_ops = 0
        self.n_waits = 0
        for e in self.ENGS:
            self.count[("c", e)] = 0
        self.dma_idx = {}
        self.dma_snap = {}
        for e in self.DMAQ:
            self.dma_idx[e] = 0
            for i in range(self.NDMASEM):
                self.count[("d", e, i)] = 0

    def op(self, eng, body, reads=(), writes=(), dma=False, rmw=()):
        reads = list(reads) + list(rmw)
        writes = list(writes) + list(rmw)
        waits = {}
        def need(k, v):
            if self.seen[eng].get(k, 0) >= v or waits.get(k, 0) >= v:
                return
            waits[k] = v
            inh = None
            if k[0] == "c" and v == self.count[k]:
                inh = self.seen[k[1]]
            elif k[0] == "d" and v == self.count[k]:
                inh = self.dma_snap.get(k)
            if inh is not None:
                mine = self.seen[eng]
                for k2, v2 in inh.items():
                    if mine.get(k2, 0) < v2:
                        mine[k2] = v2
                dead = [k3 for k3, v3 in waits.items()
                        if k3 != k and mine.get(k3, 0) >= v3]
                for k3 in dead:
                    del waits[k3]
        for b in reads:
            for k, v in b.writers.items():
                need(k, v)
        for b in writes:
            for k, v in b.writers.items():
                need(k, v)
            for k, v in b.readers.items():
                need(k, v)
        if dma:
            slot = self.dma_idx[eng] % self.NDMASEM
            self.dma_idx[eng] += 1
            key = ("d", eng, slot)
            if self.count[key] > 0:
                need(key, self.count[key])
        else:
            key = ("c", eng)
        self.count[key] += 16 if dma else 1
        ms = self.count[key]
        if dma:
            snap = dict(self.seen[eng])
            for k2, v2 in waits.items():
                if snap.get(k2, 0) < v2:
                    snap[k2] = v2
            self.dma_snap[key] = snap
        wl = sorted(waits.items())
        for k, v in wl:
            self.seen[eng][k] = max(self.seen[eng].get(k, 0), v)
        self.stream[eng].append((wl, body, (key, 16 if dma else 1)))
        self.n_ops += 1
        self.n_waits += len(wl)
        for b in reads:
            b.readers[key] = ms
        for b in writes:
            b.writers = {key: ms}
            b.readers = {}
        return ms

    def barrier(self, dummy=None, ones_bf=None):
        dma_wl = []
        for e in self.DMAQ:
            for i in range(self.NDMASEM):
                k = ("d", e, i)
                if self.count[k] > 0:
                    dma_wl.append((k, self.count[k]))
        vals = {}
        for e in self.ENGS:
            key = ("c", e)
            self.count[key] += 1
            vals[e] = self.count[key]
            wl = [(k, v) for k, v in dma_wl if self.seen[e].get(k, 0) < v]
            for k, v in wl:
                self.seen[e][k] = v
            if dummy is not None and e == "vector":
                body = (lambda g: g.memset(dummy[0:1, 0:4], 0.0))
            elif dummy is not None and e == "gpsimd":
                body = (lambda g: g.memset(dummy[0:1, 4:8], 0.0))
            elif dummy is not None and e == "scalar":
                body = (lambda g: g.memzero(dummy[0:1, 8:12]))
            elif ones_bf is not None and e == "tensor":
                body = (lambda g: g.ldweights(ones_bf[:]))
            else:
                body = (lambda g: g.nop())
            self.stream[e].append((wl, body, (key, 1)))
            self.n_ops += 1
        for e in self.ENGS:
            wl = [(("c", o), vals[o]) for o in self.ENGS if o != e]
            self.stream[e].append((wl, None, None))
            for k, v in wl:
                self.seen[e][k] = max(self.seen[e].get(k, 0), v)

    def build(self):
        nc = self.nc

        class EngProxy:
            __slots__ = ("_e", "first")
            def __init__(self, e):
                self._e = e
                self.first = None
            def __getattr__(self, a):
                f = getattr(self._e, a)
                if callable(f):
                    px = self
                    def wrap(*args, **kw):
                        r = f(*args, **kw)
                        if px.first is None and isinstance(r, bass.BassInstruction):
                            px.first = r
                        return r
                    return wrap
                return f

        es = contextlib.ExitStack()
        sems = {}
        for e in self.ENGS:
            sems[("c", e)] = es.enter_context(nc.semaphore("sc_" + e))
        for e in self.DMAQ:
            for i in range(Prog.NDMASEM):
                sems[("d", e, i)] = es.enter_context(nc.semaphore(f"sd_{e}{i}"))
        with es:
            with nc.Block() as blk:
                def runner(e):
                    def run(engif):
                        for wl, body, inc in self.stream[e]:
                            if body is None:
                                for k, v in wl:
                                    engif.wait_ge(sems[k], v)
                                continue
                            for k, v in wl[:-1]:
                                engif.wait_ge(sems[k], v)
                            px = EngProxy(engif)
                            ins = body(px)
                            if ins is None:
                                ins = engif.nop()
                            if wl:
                                k, v = wl[-1]
                                tgt = px.first if px.first is not None else ins
                                tgt._wait_ge(sems[k], v)
                            if inc is not None:
                                ins.then_inc(sems[inc[0]], inc[1])
                    return run
                blk.tensor(runner("tensor"))
                blk.vector(runner("vector"))
                blk.scalar(runner("scalar"))
                blk.gpsimd(runner("gpsimd"))
                blk.sync(runner("sync"))



def blockify(wT, pad_m_to=None):
    wT = np.asarray(wT, dtype=np.float32)
    Kd, Md = wT.shape
    if pad_m_to and Md < pad_m_to:
        wT = np.concatenate([wT, np.zeros((Kd, pad_m_to - Md), np.float32)], axis=1)
        Md = pad_m_to
    assert Kd % 128 == 0 and Md % 128 == 0
    nk, nm = Kd // 128, Md // 128
    b = wT.reshape(nk, 128, nm, 128).transpose(2, 0, 1, 3)
    return np.ascontiguousarray(b, dtype=np.float32)

def colsify(v):
    v = np.asarray(v, dtype=np.float32)
    n = v.shape[0] // 128
    return np.ascontiguousarray(v.reshape(n, 128).T, dtype=np.float32)


def host_constants():
    c = {}
    c["c_ones"] = np.ones((128, 128), dtype=np.float32)
    c["c_onesrow"] = np.ones((1, 128), dtype=np.float32)
    c["c_ident"] = np.eye(128, dtype=np.float32)
    c["c_identf"] = np.eye(128, dtype=np.float32)
    p = np.arange(128, dtype=np.float32)
    c["c_diagkeep"] = (p[None, :] >= p[:, None]).astype(np.float32)
    c["c_maskneg"] = np.where(p[None, :] >= p[:, None], 0.0, -1e30).astype(np.float32)
    c["c_iota"] = p[:, None].copy()
    return c



class Kern:
    def __init__(self):
        self.nc = bass.Bass("TRN2", target_bir_lowering=False)
        self.prog = Prog(self.nc)
        self.in_aps = {}
        self.dram_bufs = {}
        self.es = contextlib.ExitStack()
        self.sb = {}
        self.bufs = {}
        self.np_inputs = {}
        self.bf = {}

    def din(self, name, arr=None):
        if name in self.in_aps:
            return self.in_aps[name]
        assert arr is not None, name
        arr = np.ascontiguousarray(arr, dtype=np.float32)
        ap = self.nc.dram_tensor(name, list(arr.shape), F32, kind="ExternalInput").ap()
        self.in_aps[name] = ap
        self.np_inputs[name] = arr
        self.dram_bufs[name] = Buf(name)
        return ap

    def dout(self, name, shape, dt=F32):
        ap = self.nc.dram_tensor(name, list(shape), dt, kind="ExternalOutput").ap()
        self.dram_bufs[name] = Buf(name)
        return ap

    def dtmp(self, name, shape, dt):
        ap = self.nc.dram_tensor(name, list(shape), dt).ap()
        self.dram_bufs[name] = Buf(name)
        return ap

    _uid = 0

    def sbuf(self, name, shape, dt, es=None):
        Kern._uid += 1
        h = (es or self.es).enter_context(
            self.nc.sbuf_tensor(f"{name}.{Kern._uid}", list(shape), dt))
        b = Buf(name)
        if es is None:
            self.sb[name] = h
            self.bufs[name] = b
        return h, b

    def psum(self, name, shape, dt, es=None):
        Kern._uid += 1
        h = (es or self.es).enter_context(
            self.nc.psum_tensor(f"{name}.{Kern._uid}", list(shape), dt))
        return h, Buf(name)

    def load(self, dst_ap, src_name_or_ap, dst_buf, src_buf=None, eng="sync"):
        if isinstance(src_name_or_ap, str):
            src_ap = self.in_aps[src_name_or_ap]
            src_buf = self.dram_bufs[src_name_or_ap]
        else:
            src_ap = src_name_or_ap
        self.prog.op(eng, (lambda d, s: (lambda g: g.dma_start(out=d, in_=s)))(dst_ap, src_ap),
                     reads=[src_buf] if src_buf is not None else [],
                     writes=[dst_buf], dma=True)

    def store(self, dst_ap, src_ap, dst_buf, src_buf, eng="sync"):
        self.prog.op(eng, (lambda d, s: (lambda g: g.dma_start(out=d, in_=s)))(dst_ap, src_ap),
                     reads=[src_buf], writes=[dst_buf], dma=True)

    def precast(self, names):
        for name in names:
            if name in self.bf:
                continue
            arr = self.np_inputs[name]
            nel = int(np.prod(arr.shape))
            bf_ap = self.dtmp(name + "!bf", list(arr.shape), BF16)
            dims = " ".join(chr(97 + i) for i in range(arr.ndim))
            flat = "(" + dims + ")"
            sflat = self.in_aps[name].rearrange(f"{dims} -> {flat}")
            dflat = bf_ap.rearrange(f"{dims} -> {flat}")
            CH = 4 * 1024 * 1024
            for o in range(0, nel, CH):
                n = min(CH, nel - o)
                self.prog.op("gpsimd",
                             (lambda s, d: (lambda g: g.dma_start(out=d, in_=s)))(
                                 sflat[o:o + n], dflat[o:o + n]),
                             reads=[self.dram_bufs[name]],
                             writes=[self.dram_bufs[name + "!bf"]], dma=True)
            self.bf[name] = bf_ap
        return self.bf

    def bfb(self, name):
        return self.dram_bufs[name + "!bf"]

    def setup_persistent(self, consts):
        for k, v in consts.items():
            self.din(k, v)
        for k in range(NCH):
            self.sbuf(f"x{k}", [128, SEQ], F32)
        self.sbuf("s_ones_bf", [128, 128], BF16)
        self.sbuf("s_onesrow", [1, 128], F32)
        self.sbuf("s_ident_bf", [128, 128], BF16)
        self.sbuf("s_dummy", [1, 16], F32)
        self.load(self.sb["s_ones_bf"][:], "c_ones", self.bufs["s_ones_bf"], eng="gpsimd")
        self.load(self.sb["s_onesrow"][:], "c_onesrow", self.bufs["s_onesrow"])
        self.load(self.sb["s_ident_bf"][:], "c_ident", self.bufs["s_ident_bf"], eng="gpsimd")

    def load_x(self):
        xin = self.in_aps["xin"]
        for k in range(NCH):
            self.load(self.sb[f"x{k}"][:], xin[k], self.bufs[f"x{k}"],
                      self.dram_bufs["xin"])

    def store_x(self, name="xout"):
        xo = self.dram_bufs[name]
        ap = self._xout_ap
        for k in range(NCH):
            self.store(ap[k], self.sb[f"x{k}"][:], xo, self.bufs[f"x{k}"])

    def final_wait(self):
        self.prog.op("sync", lambda g: g.nop(), reads=[self.dram_bufs["xout"]])



def emit_ln_slice(K, ph, t0, T, xn_views, xn_buf):
    P = K.prog
    cast_t, cast_b = ph["cast"]
    sq_t, sq_b = ph["sq"]
    row_t, row_b = ph["row"]
    tmp_t, tmp_b = ph["tmp"]
    ones_t, ones_b = K.sb["s_ones_bf"], K.bufs["s_ones_bf"]
    onesrow_t, onesrow_b = K.sb["s_onesrow"], K.bufs["s_onesrow"]
    ps_s, ps_s_b = ph["ps_ln0"]
    ps_q, ps_q_b = ph["ps_ln1"]
    ps_m, ps_m_b = ph["ps_ln2"]
    ps_r, ps_r_b = ph["ps_ln3"]
    for k in range(NCH):
        xs = K.sb[f"x{k}"][:, t0:t0 + T]
        xsb = K.bufs[f"x{k}"]
        P.op("vector", (lambda a, b: (lambda g: g.tensor_copy(a, b)))(cast_t[:, 0:T], xs),
             reads=[xsb], writes=[cast_b])
        P.op("scalar", (lambda a, b: (lambda g: g.activation(a, b, AF.Square)))(sq_t[:, 0:T], xs),
             reads=[xsb], writes=[sq_b])
        P.op("tensor", (lambda o, r, first, last: (lambda g: g.matmul(
                o, ones_t[:, 0:1], r, start=first, stop=last)))(
                ps_s[0:1, 0:T], cast_t[:, 0:T], k == 0, k == NCH - 1),
             reads=[ones_b, cast_b], writes=[ps_s_b])
        P.op("tensor", (lambda o, r, first, last: (lambda g: g.matmul(
                o, ones_t[:, 0:1], r, start=first, stop=last)))(
                ps_q[0:1, 0:T], sq_t[:, 0:T], k == 0, k == NCH - 1),
             reads=[ones_b, sq_b], writes=[ps_q_b])
    P.op("vector", lambda g: g.tensor_single_scalar(row_t[0:1, 0:T], ps_s[0:1, 0:T],
                                                    -1.0 / D, ALU.mult),
         reads=[ps_s_b], writes=[row_b])
    P.op("scalar", lambda g: g.activation(row_t[0:1, T:2 * T], row_t[0:1, 0:T], AF.Square),
         rmw=[row_b])
    P.op("vector", lambda g: g.scalar_tensor_tensor(row_t[0:1, T:2 * T], ps_q[0:1, 0:T],
                                                    1.0 / D, row_t[0:1, T:2 * T],
                                                    ALU.mult, ALU.subtract),
         reads=[ps_q_b], rmw=[row_b])
    P.op("vector", lambda g: g.tensor_single_scalar(row_t[0:1, T:2 * T], row_t[0:1, T:2 * T],
                                                    float(EPS_LN), ALU.add),
         rmw=[row_b])
    P.op("scalar", lambda g: g.activation(row_t[0:1, T:2 * T], row_t[0:1, T:2 * T],
                                          AF.Sqrt),
         rmw=[row_b])
    P.op("vector", lambda g: g.reciprocal(row_t[0:1, T:2 * T], row_t[0:1, T:2 * T]),
         rmw=[row_b])
    P.op("tensor", lambda g: g.matmul(ps_m[:, 0:T], onesrow_t[:], row_t[0:1, 0:T],
                                      start=True, stop=True),
         reads=[onesrow_b, row_b], writes=[ps_m_b])
    P.op("tensor", lambda g: g.matmul(ps_r[:, 0:T], onesrow_t[:], row_t[0:1, T:2 * T],
                                      start=True, stop=True),
         reads=[onesrow_b, row_b], writes=[ps_r_b])
    for k in range(NCH):
        xs = K.sb[f"x{k}"][:, t0:t0 + T]
        xsb = K.bufs[f"x{k}"]
        P.op("vector", (lambda a, b, c: (lambda g: g.tensor_add(a, b, c)))(
            tmp_t[:, 0:T], xs, ps_m[:, 0:T]), reads=[xsb, ps_m_b], writes=[tmp_b])
        P.op("vector", (lambda a, b, c: (lambda g: g.tensor_mul(a, b, c)))(
            xn_views[k], tmp_t[:, 0:T], ps_r[:, 0:T]), reads=[tmp_b, ps_r_b], writes=[xn_buf])



def stage_ffn_inputs(K, pref, w):
    K.din(pref + "_w12blk", np.concatenate(
        [blockify(w["w1_w"].T), blockify(w["w2_w"].T)], axis=1))
    K.din(pref + "_h1blk", blockify(w["h1_w"].T))
    K.din(pref + "_h2blk", blockify(w["h2_w"].T))
    K.din(pref + "_w3blk", blockify(w["w3_w"].T))
    K.din(pref + "_s1c", colsify(w["w1_s"]))
    K.din(pref + "_s2c", colsify(w["w2_s"]))
    K.din(pref + "_h1sc", colsify(w["h1_s"]))
    K.din(pref + "_h2sc", colsify(w["h2_s"]))
    K.din(pref + "_s3c", colsify(w["w3_s"]))
    return [pref + "_w12blk", pref + "_h1blk", pref + "_h2blk", pref + "_w3blk"]


def emit_ffn_ternary(K, pref):
    P = K.prog
    es = contextlib.ExitStack()
    T = 512
    h_a, h_a_b = K.sbuf("ffn_ha", [128, 32 * T], BF16, es)
    h_b, h_b_b = K.sbuf("ffn_hb", [128, 32 * T], BF16, es)
    wA, wA_b = K.sbuf("ffn_wA", [128, 8 * 128], BF16, es)
    wB, wB_b = K.sbuf("ffn_wB", [128, 8 * 128], BF16, es)
    ph = {}
    ph["cast"] = K.sbuf("ffn_cast", [128, T], BF16, es)
    ph["sq"] = K.sbuf("ffn_sq", [128, T], BF16, es)
    ph["row"] = K.sbuf("ffn_row", [1, 2 * T], F32, es)
    ph["tmp"] = K.sbuf("ffn_tmp", [128, T], F32, es)
    for i, nm in enumerate(["ps_ln0", "ps_ln1", "ps_ln2", "ps_ln3"]):
        ph[nm] = K.psum(f"ffn_{nm}", [128, 512], F32, es)
    psS = [K.psum(f"ffn_s{i}", [128, 512], F32, es) for i in range(4)]
    scol, scol_b = K.sbuf("ffn_scol", [128, 5 * 32 + 16], F32, es)
    silu_t, silu_b = ph["tmp"]

    K.load(scol[:, 0:32], pref + "_s1c", scol_b)
    K.load(scol[:, 32:64], pref + "_s2c", scol_b)
    K.load(scol[:, 64:96], pref + "_h1sc", scol_b)
    K.load(scol[:, 96:128], pref + "_h2sc", scol_b)
    K.load(scol[:, 128:136], pref + "_s3c", scol_b)
    P.op("vector", lambda g: g.tensor_single_scalar(scol[:, 136:144], scol[:, 128:136],
                                                    0.5, ALU.mult), rmw=[scol_b])

    w12 = K.bf[pref + "_w12blk"]; w12_b = K.bfb(pref + "_w12blk")
    h1w = K.bf[pref + "_h1blk"]; h1w_b = K.bfb(pref + "_h1blk")
    h2w = K.bf[pref + "_h2blk"]; h2w_b = K.bfb(pref + "_h2blk")
    w3w = K.bf[pref + "_w3blk"]; w3w_b = K.bfb(pref + "_w3blk")

    def wslot(i):
        return (wA, wA_b) if i % 2 == 0 else (wB, wB_b)

    NSL = SEQ // T
    if DEV_SLICES is not None:
        NSL = DEV_SLICES
    for sl in range(NSL):
        t0 = sl * T
        xn_views = [h_b[:, k * T:(k + 1) * T] for k in range(NCH)]
        emit_ln_slice(K, ph, t0, T, xn_views, h_b_b)
        for m in range(32):
            K.load(wA[:].rearrange("p (k j) -> p k j", k=8), w12[m, 0:8].rearrange("k p j -> p k j"), wA_b, w12_b)
            K.load(wB[:].rearrange("p (k j) -> p k j", k=8), w12[m, 8:16].rearrange("k p j -> p k j"), wB_b, w12_b)
            bank_a, bank_a_b = psS[(m % 2) * 2 + 0]
            bank_b, bank_b_b = psS[(m % 2) * 2 + 1]
            def mk_group(bank, wb, koff, nk, src, sT, first=True, last=True):
                def body(g):
                    ins = None
                    for k in range(nk):
                        ins = g.matmul(bank[:, 0:sT], wb[:, (koff + k) * 128:(koff + k + 1) * 128],
                                       src[:, k * sT:(k + 1) * sT],
                                       start=(first and k == 0), stop=(last and k == nk - 1))
                    return ins
                return body
            P.op("tensor", mk_group(bank_a, wA, 0, NCH, h_b, T),
                 reads=[wA_b, h_b_b], writes=[bank_a_b])
            P.op("tensor", mk_group(bank_b, wB, 0, NCH, h_b, T),
                 reads=[wB_b, h_b_b], writes=[bank_b_b])
            P.op("scalar", (lambda bank, mm: (lambda g: g.activation(
                    silu_t[:, 0:T], bank[:, 0:T], AF.Silu, scale=scol[:, mm:mm + 1])))(bank_a, m),
                 reads=[bank_a_b, scol_b], writes=[silu_b])
            P.op("vector", (lambda bank, mm, dst: (lambda g: g.scalar_tensor_tensor(
                    dst, bank[:, 0:T], scol[:, 32 + mm:32 + mm + 1], silu_t[:, 0:T],
                    ALU.mult, ALU.mult)))(bank_b, m, h_a[:, m * T:(m + 1) * T]),
                 reads=[bank_b_b, scol_b, silu_b], writes=[h_a_b])
        for stage, (wsrc, wsrc_b, src, src_b, dst, dst_b, sc0) in enumerate([
                (h1w, h1w_b, h_a, h_a_b, h_b, h_b_b, 64),
                (h2w, h2w_b, h_b, h_b_b, h_a, h_a_b, 96)]):
            for m in range(32):
                bank, bank_b = psS[stage]
                for half in range(4):
                    wbuf, wbuf_b = wslot(m * 4 + half)
                    K.load(wbuf[:].rearrange("p (k j) -> p k j", k=8), wsrc[m, half * 8:(half + 1) * 8].rearrange("k p j -> p k j"),
                           wbuf_b, wsrc_b)
                    def mk_g(bank, wb, koff, first, last, srct):
                        def body(g):
                            ins = None
                            for k in range(8):
                                ins = g.matmul(bank[:, 0:T], wb[:, k * 128:(k + 1) * 128],
                                               srct[:, (koff + k) * T:(koff + k + 1) * T],
                                               start=(first and k == 0), stop=(last and k == 7))
                            return ins
                        return body
                    P.op("tensor", mk_g(bank, wbuf, half * 8, half == 0, half == 3, src),
                         reads=[wbuf_b, src_b], writes=[bank_b])
                P.op("scalar", (lambda bank, mm, dst_ap: (lambda g: g.activation(
                        dst_ap, bank[:, 0:T], AF.Gelu, scale=scol[:, mm:mm + 1])))(
                        bank, sc0 + m, dst[:, m * T:(m + 1) * T]),
                     reads=[bank_b, scol_b], writes=[dst_b])
        for m in range(8):
            bank, bank_b = psS[2 + m % 2]
            for half in range(4):
                wbuf, wbuf_b = wslot(m * 4 + half)
                K.load(wbuf[:].rearrange("p (k j) -> p k j", k=8), w3w[m, half * 8:(half + 1) * 8].rearrange("k p j -> p k j"),
                       wbuf_b, w3w_b)
                def mk_g(bank, wb, koff, first, last):
                    def body(g):
                        ins = None
                        for k in range(8):
                            ins = g.matmul(bank[:, 0:T], wb[:, k * 128:(k + 1) * 128],
                                           h_a[:, (koff + k) * T:(koff + k + 1) * T],
                                           start=(first and k == 0), stop=(last and k == 7))
                        return ins
                    return body
                P.op("tensor", mk_g(bank, wbuf, half * 8, half == 0, half == 3),
                     reads=[wbuf_b, h_a_b], writes=[bank_b])
            P.op("vector", (lambda bank, mm: (lambda g: g.scalar_tensor_tensor(
                    K.sb[f"x{mm}"][:, t0:t0 + T], bank[:, 0:T], scol[:, 136 + mm:136 + mm + 1],
                    K.sb[f"x{mm}"][:, t0:t0 + T], ALU.mult, ALU.add)))(bank, m),
                 reads=[bank_b, scol_b], rmw=[K.bufs[f"x{m}"]])
    P.barrier(K.sb["s_dummy"], K.sb["s_ones_bf"])
    es.close()



ATTN_CFGS = {
    "attn_mha": dict(n_head=16, n_kv=16, rope="split", causal=True, alibi=False, bias=True),
    "attn_gqa": dict(n_head=16, n_kv=4, rope="inter", causal=True, alibi=False, bias=False),
    "attn_alibi": dict(n_head=16, n_kv=4, rope="split", causal=False, alibi=True, bias=False),
}

SUB_SEQ = [
    ("ffn_t", "L0_ffn"), ("attn_mha", "L0_attn"), ("ffn_t", "L0_ffn"),
    ("ffn_lr", "L1_ffn"), ("mamba", "L1_attn"), ("ffn_lr", "L1_ffn"),
    ("ffn_cp", "L2_ffn"), ("attn_gqa", "L2_attn"), ("ffn_cp", "L2_ffn"),
    ("ffn_t", "L3_ffn"), ("attn_alibi", "L3_attn"), ("ffn_t", "L3_ffn"),
]


def stage_all_inputs(K, inputs, subset):
    consts = host_constants()
    K.din("xin", np.zeros((NCH, 128, SEQ), np.float32))
    K._xout_ap = K.dout("xout", [NCH, 128, SEQ], F32)
    K.setup_persistent(consts)
    precast = []
    done = set()
    for kind, pref in subset:
        if pref in done:
            continue
        done.add(pref)
        if kind == "ffn_t":
            w = {k[len(pref) + 1:]: inputs[k] for k in inputs if k.startswith(pref + "_")}
            precast += stage_ffn_inputs(K, pref, w)
        elif kind in ("attn_mha", "attn_gqa", "attn_alibi"):
            precast += stage_attn_inputs(K, pref, ATTN_CFGS[kind], inputs)
        elif kind == "ffn_lr":
            precast += stage_ffn_lr_inputs(K, pref, inputs)
        elif kind == "ffn_cp":
            precast += stage_ffn_cp_inputs(K, pref, inputs)
        elif kind == "mamba":
            precast += stage_mamba_inputs(K, pref, inputs)
    return precast


def build_program(inputs, subset):
    K = Kern()
    precast = stage_all_inputs(K, inputs, subset)
    K.precast(precast)
    K.load_x()
    for kind, pref in subset:
        if kind == "ffn_t":
            emit_ffn_ternary(K, pref)
        elif kind in ("attn_mha", "attn_gqa", "attn_alibi"):
            emit_attn(K, pref, ATTN_CFGS[kind])
        elif kind == "ffn_lr":
            emit_ffn_lr(K, pref)
        elif kind == "ffn_cp":
            emit_ffn_cp(K, pref)
        elif kind == "mamba":
            emit_mamba(K, pref)
        else:
            raise NotImplementedError(kind)
    K.store_x()
    K.final_wait()
    K.prog.build()
    return K


FUSED = True

def _launch(inputs, subset, xcur):
    K = build_program(inputs, subset)
    in_maps = []
    for b in range(N_CORES):
        m = dict(K.np_inputs)
        m["xin"] = xcur[b]
        in_maps.append(m)
    res = run_bass_kernel_spmd(K.nc, in_maps, core_ids=list(range(N_CORES)),
                               trace_cores=[0])
    return [np.asarray(res.results[b]["xout"]) for b in range(N_CORES)]


def kernel(**inputs):
    x = np.asarray(inputs["L0_attn_x"], np.float32)
    xcur = [np.ascontiguousarray(x[b].T).reshape(NCH, 128, SEQ) for b in range(N_CORES)]
    if FUSED:
        xcur = _launch(inputs, SUB_SEQ, xcur)
    else:
        for sub in SUB_SEQ:
            xcur = _launch(inputs, [sub], xcur)
            xcur = [np.ascontiguousarray(xc).reshape(NCH, 128, SEQ) for xc in xcur]
    out = np.empty((N_CORES, SEQ, D), np.float32)
    for b in range(N_CORES):
        out[b] = xcur[b].reshape(D, SEQ).T
    return out



ALIBI_SLOPES = None

def _alibi_slopes16():
    global ALIBI_SLOPES
    if ALIBI_SLOPES is None:
        def pow2(n):
            start = 2.0 ** (-2.0 ** (-(math.log2(n) - 3)))
            return [start * start ** i for i in range(n)]
        ALIBI_SLOPES = pow2(16)
    return ALIBI_SLOPES


def stage_attn_inputs(K, pref, cfg, inputs):
    nkv = cfg["n_kv"]
    kvd = nkv * 64
    Wq = np.asarray(inputs[pref + "_Wq"], np.float32)
    Wk = np.asarray(inputs[pref + "_Wk"], np.float32)
    Wv = np.asarray(inputs[pref + "_Wv"], np.float32)
    Wo = np.asarray(inputs[pref + "_Wo"], np.float32)
    wqkv = np.concatenate([Wq, Wk, Wv], axis=1)
    K.din(pref + "_wqkv", np.ascontiguousarray(wqkv.reshape(NCH, 128, D + 2 * kvd)))
    K.din(pref + "_woblk", blockify(Wo))
    if cfg["bias"]:
        brow = np.concatenate([np.asarray(inputs[pref + "_bq"], np.float32),
                               np.asarray(inputs[pref + "_bk"], np.float32),
                               np.asarray(inputs[pref + "_bv"], np.float32)])
        K.din(pref + "_bqkv", np.ascontiguousarray(brow.reshape(-1, 512)))
        K.din(pref + "_boc", colsify(np.asarray(inputs[pref + "_bo"], np.float32)))
    if "c_ropecos16" not in K.np_inputs:
        dim = 64
        inv_freq = np.exp(np.arange(0, dim, 2, dtype=np.float32) * (-np.log(10000.0) / dim))
        pos = np.arange(SEQ, dtype=np.float32)
        ang = (pos[:, None] * inv_freq[None, :]).astype(np.float32)
        cos = np.cos(ang).astype(np.float32).reshape(NTT, 128, 32)
        sin = np.sin(ang).astype(np.float32).reshape(NTT, 128, 32)
        K.din("c_ropecos16", np.ascontiguousarray(np.tile(cos, (1, 1, 16))))
        K.din("c_ropesin16", np.ascontiguousarray(np.tile(sin, (1, 1, 16))))
        p = np.arange(128, dtype=np.float32)
        K.din("c_keepmaster", (np.arange(1024)[None, :] >= (p[:, None] + 512)).astype(np.float32))
        K.din("c_kposc", (np.arange(NTT)[None, :] * 128 + p[:, None]).astype(np.float32))
    return [pref + "_wqkv", pref + "_woblk"]


def emit_attn(K, pref, cfg):
    P = K.prog
    nh, nkv = cfg["n_head"], cfg["n_kv"]
    kvd = nkv * 64
    qkvd = D + 2 * kvd
    causal, alibi = cfg["causal"], cfg["alibi"]
    T = 512
    NSL = SEQ // T
    nkch = kvd // 128

    qscr = K.dtmp(pref + "!q", [NCH, 128, SEQ], BF16)
    kscr = K.dtmp(pref + "!k", [nkch, 128, SEQ], BF16)
    vscr = K.dtmp(pref + "!v", [NTT, 128, kvd], BF16)
    qscr_b, kscr_b, vscr_b = (K.dram_bufs[pref + "!q"], K.dram_bufs[pref + "!k"],
                              K.dram_bufs[pref + "!v"])
    wqkv_bf = K.bf[pref + "_wqkv"]; wqkv_bfb = K.bfb(pref + "_wqkv")
    wo_bf = K.bf[pref + "_woblk"]; wo_bfb = K.bfb(pref + "_woblk")

    es = contextlib.ExitStack()
    ph = {}
    ph["cast"] = K.sbuf("a1_cast", [128, T], BF16, es)
    ph["sq"] = K.sbuf("a1_sq", [128, T], BF16, es)
    ph["row"] = K.sbuf("a1_row", [1, 2 * T], F32, es)
    tmpf, tmpf_b = K.sbuf("a1_tmp", [128, 1024], F32, es)
    ph["tmp"] = (tmpf, tmpf_b)
    rp1, rp1_b = K.sbuf("a1_rp1", [128, 512], F32, es)
    rp2, rp2_b = K.sbuf("a1_rp2", [128, 512], F32, es)
    xn_t, xn_b = K.sbuf("a1_xn", [128, NCH * T], BF16, es)
    wq1, wq1_b = K.sbuf("a1_wq1", [128, qkvd], BF16, es)
    wq2, wq2_b = K.sbuf("a1_wq2", [128, qkvd], BF16, es)
    crep, crep_b = K.sbuf("a1_crep", [128, 4 * 512], F32, es)
    srep, srep_b = K.sbuf("a1_srep", [128, 4 * 512], F32, es)
    ro_st, ro_st_b = K.sbuf("a1_rost", [128, D + kvd], BF16, es)
    v_st, v_st_b = K.sbuf("a1_vst", [128, kvd], BF16, es)
    tr_st, tr_st_b = K.sbuf("a1_trst", [128, D + kvd], BF16, es)
    psp = [K.psum(f"a1_p{i}", [128, 512], F32, es) for i in range(6)]
    for i in range(4):
        ph[f"ps_ln{i}"] = psp[(2 + i) % 6]
    ptr = [K.psum(f"a1_ptr{i}", [128, 1024], BF16, es) for i in range(2)]
    if cfg["bias"]:
        bias_bc, bias_bc_b = K.sbuf("a1_biasb", [128, qkvd], BF16, es)
        brow_t, brow_b = K.sbuf("a1_brow", [1, 512], F32, es)
        for j in range(qkvd // 512):
            K.load(brow_t[:], K.in_aps[pref + "_bqkv"][j:j + 1], brow_b,
                   K.dram_bufs[pref + "_bqkv"])
            bk, bk_b = psp[0]
            P.op("tensor", (lambda o, r: (lambda g: g.matmul(o, K.sb["s_onesrow"][:], r,
                                                             start=True, stop=True)))(
                 bk[:], brow_t[0:1, :]),
                 reads=[K.bufs["s_onesrow"], brow_b], writes=[bk_b])
            P.op("vector", (lambda d, s: (lambda g: g.tensor_copy(d, s)))(
                 bias_bc[:, j * 512:(j + 1) * 512], bk[:]), reads=[bk_b], writes=[bias_bc_b])

    def rope_emit(part_off, wdt, dst0, tt):
        nhh = wdt // 64
        if cfg["rope"] == "split":
            sl1, sl2 = (0, 32), (32, 64)
            step = 1
        else:
            sl1, sl2 = (0, 64), (1, 64)
            step = 2
        def view(tens, a):
            v3 = tens[:, 0:wdt].rearrange("p (h f) -> p h f", f=64)
            if step == 1:
                return v3[:, :, a[0]:a[1]]
            return v3[:, :, a[0]:64:2]
        x1, x2 = view(tmpf, sl1), view(tmpf, sl2)
        def ovw(a):
            v3 = ro_st[:, dst0:dst0 + wdt].rearrange("p (h f) -> p h f", f=64)
            if step == 1:
                return v3[:, :, a[0]:a[1]]
            return v3[:, :, a[0]:64:2]
        o1, o2 = ovw(sl1), ovw(sl2)
        ct = crep[:, tt * 512:tt * 512 + nhh * 32].rearrange("p (h f) -> p h f", f=32)
        st = srep[:, tt * 512:tt * 512 + nhh * 32].rearrange("p (h f) -> p h f", f=32)
        r1 = rp1[:, 0:nhh * 32].rearrange("p (h f) -> p h f", f=32)
        r2 = rp2[:, 0:nhh * 32].rearrange("p (h f) -> p h f", f=32)
        P.op("vector", (lambda a, b, c: (lambda g: g.tensor_mul(a, b, c)))(r1, x1, ct),
             reads=[tmpf_b, crep_b], writes=[rp1_b])
        P.op("vector", (lambda a, b, c: (lambda g: g.tensor_mul(a, b, c)))(r2, x2, st),
             reads=[tmpf_b, srep_b], writes=[rp2_b])
        P.op("vector", (lambda a, b, c: (lambda g: g.tensor_sub(a, b, c)))(o1, r1, r2),
             reads=[rp1_b, rp2_b], writes=[ro_st_b])
        P.op("vector", (lambda a, b, c: (lambda g: g.tensor_mul(a, b, c)))(r1, x1, st),
             reads=[tmpf_b, srep_b], writes=[rp1_b])
        P.op("vector", (lambda a, b, c: (lambda g: g.tensor_mul(a, b, c)))(r2, x2, ct),
             reads=[tmpf_b, crep_b], writes=[rp2_b])
        P.op("vector", (lambda a, b, c: (lambda g: g.tensor_add(a, b, c)))(o2, r1, r2),
             reads=[rp1_b, rp2_b], writes=[ro_st_b])

    nbank = qkvd // 512
    for sl in range(NSL):
        t0 = sl * T
        xn_views = [xn_t[:, k * T:(k + 1) * T] for k in range(NCH)]
        emit_ln_slice(K, ph, t0, T, xn_views, xn_b)
        K.load(crep[:].rearrange("p (t f) -> p t f", t=4),
               K.in_aps["c_ropecos16"][sl * 4:(sl + 1) * 4].rearrange("t p f -> p t f"),
               crep_b, K.dram_bufs["c_ropecos16"])
        K.load(srep[:].rearrange("p (t f) -> p t f", t=4),
               K.in_aps["c_ropesin16"][sl * 4:(sl + 1) * 4].rearrange("t p f -> p t f"),
               srep_b, K.dram_bufs["c_ropesin16"])
        for tt in range(4):
            t = sl * 4 + tt
            for k in range(NCH):
                wbuf, wbuf_b = (wq1, wq1_b) if k % 2 == 0 else (wq2, wq2_b)
                K.load(wbuf[:], wqkv_bf[k], wbuf_b, wqkv_bfb)
                def mkp(wb, k, tt):
                    def body(g):
                        ins = None
                        for bi in range(nbank):
                            ins = g.matmul(psp[bi][0][:],
                                           xn_t[:, k * T + tt * 128:k * T + (tt + 1) * 128],
                                           wb[:, bi * 512:(bi + 1) * 512],
                                           start=(k == 0), stop=(k == NCH - 1))
                        return ins
                    return body
                P.op("tensor", mkp(wbuf, k, tt),
                     reads=[wbuf_b, xn_b] + ([] if k > 0 else []),
                     writes=[psp[bi][1] for bi in range(nbank)])
            def segs(off, wdt):
                out = []
                o = off
                while o < off + wdt:
                    n = min(512 - o % 512, off + wdt - o)
                    out.append((o // 512, o % 512, o - off, n))
                    o += n
                return out
            for part, (j0, wdt) in enumerate([(0, D), (D, kvd)]):
                for (bi, bc0, d0, n) in segs(j0, wdt):
                    bank, bank_b = psp[bi]
                    if cfg["bias"]:
                        P.op("vector", (lambda d, s, bb: (lambda g: g.tensor_add(d, s, bb)))(
                             tmpf[:, d0:d0 + n], bank[:, bc0:bc0 + n],
                             bias_bc[:, j0 + d0:j0 + d0 + n]),
                             reads=[bank_b, bias_bc_b], writes=[tmpf_b])
                    else:
                        P.op("vector", (lambda d, s: (lambda g: g.tensor_copy(d, s)))(
                             tmpf[:, d0:d0 + n], bank[:, bc0:bc0 + n]),
                             reads=[bank_b], writes=[tmpf_b])
                rope_emit(j0, wdt, 0 if part == 0 else D, tt)
            for (bi, bc0, d0, n) in segs(D + kvd, kvd):
                bank, bank_b = psp[bi]
                if cfg["bias"]:
                    P.op("vector", (lambda d, s, bb: (lambda g: g.scalar_tensor_tensor(
                         d, s, 1.0, bb, ALU.mult, ALU.add)))(
                         v_st[:, d0:d0 + n], bank[:, bc0:bc0 + n],
                         bias_bc[:, D + kvd + d0:D + kvd + d0 + n]),
                         reads=[bank_b, bias_bc_b], writes=[v_st_b])
                else:
                    P.op("vector", (lambda d, s: (lambda g: g.tensor_copy(d, s)))(
                         v_st[:, d0:d0 + n], bank[:, bc0:bc0 + n]),
                         reads=[bank_b], writes=[v_st_b])
            K.store(vscr[t], v_st[:, 0:kvd], vscr_b, v_st_b)
            ntch = (D + kvd) // 128
            for c in range(ntch):
                pbank = ptr[c % 2]
                ph_, pb_ = pbank
                P.op("tensor", (lambda o, i_: (lambda g: g.transpose(o, i_, K.sb["s_ident_bf"][:])))(
                     ph_[:, 0:128], ro_st[:, c * 128:(c + 1) * 128]),
                     reads=[ro_st_b, K.bufs["s_ident_bf"]], writes=[pb_])
                P.op("vector", (lambda d, s: (lambda g: g.tensor_copy(d, s)))(
                     tr_st[:, c * 128:(c + 1) * 128], ph_[:, 0:128]),
                     reads=[pb_], writes=[tr_st_b])
            K.store(qscr[:, :, t * 128:(t + 1) * 128].rearrange("c p j -> p c j"),
                    tr_st[:, 0:D].rearrange("p (c j) -> p c j", j=128), qscr_b, tr_st_b)
            K.store(kscr[:, :, t * 128:(t + 1) * 128].rearrange("c p j -> p c j"),
                    tr_st[:, D:D + kvd].rearrange("p (c j) -> p c j", j=128), kscr_b, tr_st_b)
    P.barrier(K.sb["s_dummy"], K.sb["s_ones_bf"])
    es.close()

    es = contextlib.ExitStack()
    kfm, kfm_b = K.sbuf("a2_kfm", [128, 2 * SEQ], BF16, es)
    vgrp, vgrp_b = K.sbuf("a2_vgrp", [128, NTT * 256], BF16, es)
    vaug, vaug_b = K.sbuf("a2_vaug", [128, 4 * NTT * 65], BF16, es)
    qb1, qb1_b = K.sbuf("a2_qb1", [128, 512], BF16, es)
    qb2, qb2_b = K.sbuf("a2_qb2", [128, 512], BF16, es)
    pt1, pt1_b = K.sbuf("a2_pt1", [128, 512], BF16, es)
    pt2, pt2_b = K.sbuf("a2_pt2", [128, 512], BF16, es)
    sqs, sqs_b = K.sbuf("a2_sqs", [128, 512], BF16, es)
    keepm, keepm_b = K.sbuf("a2_keep", [128, 1024], BF16, es)
    rl_row, rl_row_b = K.sbuf("a2_rl", [1, 512], F32, es)
    rlb, rlb_b = K.sbuf("a2_rlb", [128, 512], F32, es)
    nqch_blk = 2 if nkv == nh else NCH
    ctx_t, ctx_b = K.sbuf("a2_ctx", [128, nqch_blk * 512], BF16, es)
    wo_t, wo_b = K.sbuf("a2_wo", [128, 4 * NCH * 128], BF16, es)
    cols, cols_b = K.sbuf("a2_cols", [128, 48], F32, es)
    kposc, kposc_b = K.sbuf("a2_kposc", [128, NTT], F32, es)
    skp, skp_b = K.sbuf("a2_skp", [128, NTT], F32, es)
    psS = [K.psum(f"a2_s{i}", [128, 512], F32, es) for i in range(3)]
    psC, psC_b = K.psum("a2_c", [128, 512], F32, es)
    psB, psB_b = K.psum("a2_b", [128, 512], F32, es)
    psO = [K.psum(f"a2_o{i}", [128, 512], F32, es) for i in range(2)]

    K.load(keepm[:], "c_keepmaster", keepm_b, eng="gpsimd")
    K.load(kposc[:], "c_kposc", kposc_b)
    P.op("vector", lambda g: g.memset(cols[:, 0:1], 0.125), writes=[cols_b])
    P.op("vector", lambda g: g.memset(vaug[:].rearrange("p (h t j) -> p h t j", h=4, j=65)[:, :, :, 64:65], 1.0),
         writes=[vaug_b])
    if cfg["bias"]:
        boc, boc_b = K.sbuf("a2_boc", [128, 8], F32, es)
        K.load(boc[:], pref + "_boc", boc_b)

    slopes = _alibi_slopes16()
    nblk = nkv // 4 if nkv >= 4 else 1
    for blk in range(max(1, nkv // 4)):
        K.load(kfm[:, 0:SEQ], kscr[2 * blk], kfm_b, kscr_b)
        K.load(kfm[:, SEQ:2 * SEQ], kscr[2 * blk + 1], kfm_b, kscr_b)
        K.load(vgrp[:].rearrange("p (t j) -> p t j", j=256),
               vscr[:, :, blk * 256:blk * 256 + 256].rearrange("t p j -> p t j"),
               vgrp_b, vscr_b)
        for hh in range(4):
            P.op("vector", (lambda hh: (lambda g: g.tensor_copy(
                 vaug[:].rearrange("p (h t j) -> p h t j", h=4, j=65)[:, hh, :, 0:64],
                 vgrp[:].rearrange("p (t j) -> p t j", j=256)[:, :, hh * 64:(hh + 1) * 64])))(hh),
                 reads=[vgrp_b], writes=[vaug_b])
        for hh in range(4):
            pbase = 64 * (hh % 2)
            coff = (hh // 2) * SEQ
            first = True
            for j in range(0, SEQ, 512):
                P.op("scalar", (lambda a, b: (lambda g: g.activation(a, b, AF.Square)))(
                     sqs[0:64, 0:512] if pbase == 0 else sqs[64:128, 0:512],
                     kfm[pbase:pbase + 64, coff + j:coff + j + 512]),
                     reads=[kfm_b], writes=[sqs_b])
                P.op("tensor", (lambda o, r, lo: (lambda g: g.matmul(
                     o, lo, r, start=True, stop=True)))(
                     psC[:], sqs[0:64, 0:512] if pbase == 0 else sqs[64:128, 0:512],
                     K.sb["s_ones_bf"][0:64, :] if pbase == 0 else K.sb["s_ones_bf"][64:128, :]),
                     reads=[K.bufs["s_ones_bf"], sqs_b], writes=[psC_b])
                P.op("vector", (lambda hh, f: (lambda g: g.reduce_max(
                     cols[:, 44:45], psC[:], AX.X)))(hh, first),
                     reads=[psC_b], rmw=[cols_b])
                if first:
                    P.op("vector", (lambda hh: (lambda g: g.tensor_copy(
                         cols[:, 4 + hh:5 + hh], cols[:, 44:45])))(hh), rmw=[cols_b])
                else:
                    P.op("vector", (lambda hh: (lambda g: g.tensor_max(
                         cols[:, 4 + hh:5 + hh], cols[:, 4 + hh:5 + hh], cols[:, 44:45])))(hh),
                         rmw=[cols_b])
                first = False
            P.op("scalar", (lambda hh: (lambda g: g.activation(
                 cols[:, 8 + hh:9 + hh], cols[:, 4 + hh:5 + hh], AF.Sqrt)))(hh), rmw=[cols_b])
        q_heads = [h for h in range(nh)
                   if (h if nkv == nh else h // 4) in (4 * blk, 4 * blk + 1, 4 * blk + 2, 4 * blk + 3)]
        qchs = sorted({qh // 2 for qh in q_heads})
        for qb in range(NSL):
            for ih, h in enumerate(q_heads):
                qch = h // 2
                hp = 64 * (h % 2)
                hh_kv = (h if nkv == nh else h // 4) % 4
                kp = 64 * (hh_kv % 2)
                qbuf, qbuf_bb = (qb1, qb1_b) if ih % 2 == 0 else (qb2, qb2_b)
                K.load(qbuf[kp:kp + 64, :], qscr[qch, hp:hp + 64, qb * 512:(qb + 1) * 512],
                       qbuf_bb, qscr_b)
                P.op("scalar", (lambda kp, qbuf: (lambda g: g.activation(
                     sqs[kp:kp + 64, :], qbuf[kp:kp + 64, :], AF.Square)))(kp, qbuf),
                     reads=[qbuf_bb], writes=[sqs_b])
                P.op("tensor", (lambda kp: (lambda g: g.matmul(
                     psC[:], K.sb["s_ones_bf"][kp:kp + 64, :] if kp else K.sb["s_ones_bf"][0:64, :],
                     sqs[kp:kp + 64, :], start=True, stop=True)))(kp),
                     reads=[K.bufs["s_ones_bf"], sqs_b], writes=[psC_b])
                P.op("vector", lambda g: g.reduce_max(cols[:, 2:3], psC[:], AX.X),
                     reads=[psC_b], rmw=[cols_b])
                P.op("scalar", lambda g: g.activation(cols[:, 3:4], cols[:, 2:3], AF.Sqrt),
                     rmw=[cols_b])
                P.op("vector", (lambda hh_kv: (lambda g: g.tensor_scalar(
                     cols[:, 1:2], cols[:, 3:4], cols[:, 8 + hh_kv:9 + hh_kv], -0.125,
                     ALU.mult, ALU.mult)))(hh_kv), rmw=[cols_b])
                if alibi:
                    sl_h = float(slopes[h])
                    P.op("vector", (lambda sl_h: (lambda g: g.tensor_single_scalar(
                         skp[:], kposc[:], sl_h, ALU.mult)))(sl_h),
                         reads=[kposc_b], writes=[skp_b])
                    P.op("vector", lambda g: g.tensor_scalar_add(
                         cols[:, 12:12 + NTT], skp[:], cols[:, 1:2]),
                         reads=[skp_b], rmw=[cols_b])
                    P.op("vector", (lambda sl_h: (lambda g: g.tensor_single_scalar(
                         cols[:, 12:12 + NTT], cols[:, 12:12 + NTT],
                         -sl_h * (SEQ - 1.0), ALU.add)))(sl_h), rmw=[cols_b])
                kt_end = (qb + 1) * 4 if causal else NTT
                ctx_bank, ctx_bank_b = psO[ih % 2]
                kcoff = (hh_kv // 2) * SEQ
                for kt in range(kt_end):
                    sbank, sbank_b = psS[kt % 3]
                    P.op("tensor", (lambda sb_, kt, kp, kcoff, qbuf: (lambda g: g.matmul(
                         sb_[:], kfm[kp:kp + 64, kcoff + kt * 128:kcoff + (kt + 1) * 128],
                         qbuf[kp:kp + 64, :], start=True, stop=True)))(
                         sbank, kt, kp, kcoff, qbuf),
                         reads=[kfm_b, qbuf_bb], writes=[sbank_b])
                    ptile, ptile_b = (pt1, pt1_b) if kt % 2 == 0 else (pt2, pt2_b)
                    bias_ap = cols[:, 12 + kt:13 + kt] if alibi else cols[:, 1:2]
                    P.op("scalar", (lambda pt_, sb_, bias_ap: (lambda g: g.activation(
                         pt_[:], sb_[:], AF.Exp, bias=bias_ap, scale=cols[:, 0:1])))(
                         ptile, sbank, bias_ap),
                         reads=[sbank_b, cols_b], writes=[ptile_b])
                    if causal and kt >= qb * 4:
                        off = 512 - (kt - qb * 4) * 128
                        P.op("vector", (lambda pt_, off: (lambda g: g.tensor_mul(
                             pt_[:], pt_[:], keepm[:, off:off + 512])))(ptile, off),
                             reads=[keepm_b], rmw=[ptile_b])
                    P.op("tensor", (lambda cb, pt_, hh_kv, kt, first, last: (lambda g: g.matmul(
                         cb[0:65, :], vaug[:].rearrange("p (h t j) -> p h t j", h=4, j=65)[:, hh_kv, kt, :],
                         pt_[:], start=first, stop=last)))(
                         ctx_bank, ptile, hh_kv, kt, kt == 0, kt == kt_end - 1),
                         reads=[vaug_b, ptile_b], writes=[ctx_bank_b])
                P.op("vector", (lambda cb: (lambda g: g.reciprocal(rl_row[:], cb[64:65, :])))(ctx_bank),
                     reads=[ctx_bank_b], writes=[rl_row_b])
                P.op("tensor", lambda g: g.matmul(psB[0:64, :], K.sb["s_onesrow"][0:1, 0:64],
                                                  rl_row[:], start=True, stop=True),
                     reads=[K.bufs["s_onesrow"], rl_row_b], writes=[psB_b])
                P.op("vector", lambda g: g.tensor_copy(rlb[0:64, :], psB[0:64, :]),
                     reads=[psB_b], writes=[rlb_b])
                ci_ctx = qchs.index(qch)
                P.op("vector", (lambda cb, hp, ci_ctx: (lambda g: g.tensor_mul(
                     ctx_t[:, ci_ctx * 512:(ci_ctx + 1) * 512][hp:hp + 64, :],
                     cb[0:64, :], rlb[0:64, :])))(ctx_bank, hp, ci_ctx),
                     reads=[ctx_bank_b, rlb_b], writes=[ctx_b])
            for mg0 in range(0, 8, 4):
                K.load(wo_t[:].rearrange("p (mc j) -> p mc j", j=128),
                       wo_bf[mg0:mg0 + 4].rearrange("m c p j -> p (m c) j"),
                       wo_b, wo_bfb)
                for mi in range(4):
                    m = mg0 + mi
                    bank, bank_b = psS[mi % 2]
                    def mko(bank, mi, c0):
                        def body(g):
                            ins = None
                            for ci in range(nqch_blk):
                                ins = g.matmul(bank[:],
                                               wo_t[:, (mi * NCH + c0 + ci) * 128:(mi * NCH + c0 + ci + 1) * 128],
                                               ctx_t[:, ci * 512:(ci + 1) * 512],
                                               start=(ci == 0), stop=(ci == nqch_blk - 1))
                            return ins
                        return body
                    P.op("tensor", mko(bank, mi, qchs[0]), reads=[wo_b, ctx_b], writes=[bank_b])
                    if cfg["bias"] and blk == 0:
                        P.op("vector", (lambda bank, m, qb: (lambda g: g.scalar_tensor_tensor(
                             K.sb[f"x{m}"][:, qb * 512:(qb + 1) * 512], bank[:],
                             boc[:, m:m + 1], K.sb[f"x{m}"][:, qb * 512:(qb + 1) * 512],
                             ALU.add, ALU.add)))(bank, m, qb),
                             reads=[bank_b, boc_b], rmw=[K.bufs[f"x{m}"]])
                    else:
                        P.op("vector", (lambda bank, m, qb: (lambda g: g.tensor_add(
                             K.sb[f"x{m}"][:, qb * 512:(qb + 1) * 512],
                             K.sb[f"x{m}"][:, qb * 512:(qb + 1) * 512], bank[:])))(bank, m, qb),
                             reads=[bank_b], rmw=[K.bufs[f"x{m}"]])
    P.barrier(K.sb["s_dummy"], K.sb["s_ones_bf"])
    es.close()



def stage_ffn_lr_inputs(K, pref, inputs):
    K.din(pref + "_u1blk", blockify(np.asarray(inputs[pref + "_U1"], np.float32)))
    K.din(pref + "_v1blk", blockify(np.asarray(inputs[pref + "_V1"], np.float32)))
    K.din(pref + "_u2blk", blockify(np.asarray(inputs[pref + "_U2"], np.float32)))
    K.din(pref + "_v2blk", blockify(np.asarray(inputs[pref + "_V2"], np.float32)))
    K.din(pref + "_b1c", colsify(np.asarray(inputs[pref + "_b1"], np.float32)))
    K.din(pref + "_b2c", colsify(np.asarray(inputs[pref + "_b2"], np.float32)))
    return [pref + "_u1blk", pref + "_v1blk", pref + "_u2blk", pref + "_v2blk"]


def emit_ffn_lr(K, pref):
    P = K.prog
    es = contextlib.ExitStack()
    T = 512
    NSL = SEQ // T
    ph = {}
    ph["cast"] = K.sbuf("lr_cast", [128, T], BF16, es)
    ph["sq"] = K.sbuf("lr_sq", [128, T], BF16, es)
    ph["row"] = K.sbuf("lr_row", [1, 2 * T], F32, es)
    ph["tmp"] = K.sbuf("lr_tmp", [128, T], F32, es)
    xn_t, xn_b = K.sbuf("lr_xn", [128, NCH * T], BF16, es)
    H_t, H_b = K.sbuf("lr_H", [128, 32 * T], BF16, es)
    Pq_t, Pq_b = K.sbuf("lr_P", [128, 2 * T], BF16, es)
    w_t, w_b = K.sbuf("lr_w", [128, 32 * 128], BF16, es)
    wA, wA_b = K.sbuf("lr_wA", [128, 8 * 128], BF16, es)
    wB, wB_b = K.sbuf("lr_wB", [128, 8 * 128], BF16, es)
    bcol, bcol_b = K.sbuf("lr_bc", [128, 32 + 8 + 1], F32, es)
    psb = [K.psum(f"lr_p{i}", [128, 512], F32, es) for i in range(6)]
    for i in range(4):
        ph[f"ps_ln{i}"] = psb[2 + i]
    OFF_U1, OFF_V2 = 0, 16
    for nm, off in [("_u1blk", OFF_U1), ("_v2blk", OFF_V2)]:
        K.load(w_t[:, off * 128:(off + 16) * 128].rearrange("p (mc j) -> p mc j", j=128),
               K.bf[pref + nm].rearrange("m c p j -> p (m c) j"), w_b, K.bfb(pref + nm))
    K.load(bcol[:, 0:32], pref + "_b1c", bcol_b)
    K.load(bcol[:, 32:40], pref + "_b2c", bcol_b)
    P.op("vector", lambda g: g.memset(bcol[:, 40:41], 1.0), writes=[bcol_b])

    def blk(off, m, k, nk):
        j = off + m * nk + k
        return w_t[:, j * 128:(j + 1) * 128]

    for sl in range(NSL):
        t0 = sl * T
        xn_views = [xn_t[:, k * T:(k + 1) * T] for k in range(NCH)]
        emit_ln_slice(K, ph, t0, T, xn_views, xn_b)
        for m in range(2):
            bank, bank_b = psb[m]
            def mk(bank, m):
                def body(g):
                    ins = None
                    for k in range(NCH):
                        ins = g.matmul(bank[:], blk(OFF_U1, m, k, 8), xn_t[:, k * T:(k + 1) * T],
                                       start=(k == 0), stop=(k == NCH - 1))
                    return ins
                return body
            P.op("tensor", mk(bank, m), reads=[w_b, xn_b], writes=[bank_b])
            P.op("vector", (lambda bank, m: (lambda g: g.tensor_copy(
                 Pq_t[:, m * T:(m + 1) * T], bank[:])))(bank, m),
                 reads=[bank_b], writes=[Pq_b])
        for m in range(32):
            wbuf, wbuf_b = (wA, wA_b) if (m // 4) % 2 == 0 else (wB, wB_b)
            if m % 4 == 0:
                K.load(wbuf[:].rearrange("p (mc j) -> p mc j", j=128),
                       K.bf[pref + "_v1blk"][m:m + 4].rearrange("m c p j -> p (m c) j"),
                       wbuf_b, K.bfb(pref + "_v1blk"))
            bank, bank_b = psb[m % 2]
            def mk(bank, m, wbuf):
                def body(g):
                    ins = None
                    for k in range(2):
                        ins = g.matmul(bank[:], wbuf[:, ((m % 4) * 2 + k) * 128:((m % 4) * 2 + k + 1) * 128],
                                       Pq_t[:, k * T:(k + 1) * T],
                                       start=(k == 0), stop=(k == 1))
                    return ins
                return body
            P.op("tensor", mk(bank, m, wbuf), reads=[wbuf_b, Pq_b], writes=[bank_b])
            P.op("scalar", (lambda bank, m: (lambda g: g.activation(
                 H_t[:, m * T:(m + 1) * T], bank[:], AF.Gelu_apprx_tanh,
                 bias=bcol[:, m:m + 1], scale=bcol[:, 40:41])))(bank, m),
                 reads=[bank_b, bcol_b], writes=[H_b])
        for m in range(2):
            bank, bank_b = psb[m]
            for q4 in range(4):
                wbuf, wbuf_b = (wA, wA_b) if q4 % 2 == 0 else (wB, wB_b)
                K.load(wbuf[:].rearrange("p (k j) -> p k j", j=128),
                       K.bf[pref + "_u2blk"][m, q4 * 8:(q4 + 1) * 8].rearrange("k p j -> p k j"),
                       wbuf_b, K.bfb(pref + "_u2blk"))
                def mk(bank, m, q4, wbuf):
                    def body(g):
                        ins = None
                        for k in range(8):
                            ins = g.matmul(bank[:], wbuf[:, k * 128:(k + 1) * 128],
                                           H_t[:, (q4 * 8 + k) * T:(q4 * 8 + k + 1) * T],
                                           start=(q4 == 0 and k == 0), stop=(q4 == 3 and k == 7))
                        return ins
                    return body
                P.op("tensor", mk(bank, m, q4, wbuf), reads=[wbuf_b, H_b], writes=[bank_b])
            P.op("vector", (lambda bank, m: (lambda g: g.tensor_copy(
                 Pq_t[:, m * T:(m + 1) * T], bank[:])))(bank, m),
                 reads=[bank_b], writes=[Pq_b])
        for m in range(NCH):
            bank, bank_b = psb[2 + m % 2]
            def mk(bank, m):
                def body(g):
                    ins = None
                    for k in range(2):
                        ins = g.matmul(bank[:], blk(OFF_V2, m, k, 2), Pq_t[:, k * T:(k + 1) * T],
                                       start=(k == 0), stop=(k == 1))
                    return ins
                return body
            P.op("tensor", mk(bank, m), reads=[w_b, Pq_b], writes=[bank_b])
            P.op("vector", (lambda bank, m: (lambda g: g.tensor_scalar(
                 ph["tmp"][0][:, 0:T], bank[:], bcol[:, 32 + m:33 + m], 0.5, ALU.add, ALU.mult)))(bank, m),
                 reads=[bank_b, bcol_b], writes=[ph["tmp"][1]])
            P.op("vector", (lambda m, t0: (lambda g: g.tensor_add(
                 K.sb[f"x{m}"][:, t0:t0 + T], K.sb[f"x{m}"][:, t0:t0 + T],
                 ph["tmp"][0][:, 0:T])))(m, t0),
                 reads=[ph["tmp"][1]], rmw=[K.bufs[f"x{m}"]])
    P.barrier(K.sb["s_dummy"], K.sb["s_ones_bf"])
    es.close()



def stage_ffn_cp_inputs(K, pref, inputs):
    names = []
    for i, nm in enumerate(["cn_w1", "cn_w2", "cn_w3"]):
        K.din(pref + f"_{nm}blk", blockify(np.asarray(inputs[pref + "_" + nm], np.float32)))
        names.append(pref + f"_{nm}blk")
    K.din(pref + "_cn_w4blk", blockify(np.asarray(inputs[pref + "_cn_w4"], np.float32), pad_m_to=128))
    names.append(pref + "_cn_w4blk")
    for nm in ["gc_w1", "gc_w2"]:
        K.din(pref + f"_{nm}blk", blockify(np.asarray(inputs[pref + "_" + nm], np.float32)))
        names.append(pref + f"_{nm}blk")
    K.din(pref + "_gc_w3blk", blockify(np.asarray(inputs[pref + "_gc_w3"], np.float32), pad_m_to=128))
    names.append(pref + "_gc_w3blk")
    K.din(pref + "_cb1", colsify(np.asarray(inputs[pref + "_cn_b1"], np.float32)))
    K.din(pref + "_cb2", colsify(np.asarray(inputs[pref + "_cn_b2"], np.float32)))
    K.din(pref + "_cb3", colsify(np.asarray(inputs[pref + "_cn_b3"], np.float32)))
    K.din(pref + "_cb4", np.asarray(inputs[pref + "_cn_b4"], np.float32).reshape(1, 1))
    K.din(pref + "_gb1", colsify(np.asarray(inputs[pref + "_gc_b1"], np.float32)))
    K.din(pref + "_gb2", colsify(np.asarray(inputs[pref + "_gc_b2"], np.float32)))
    K.din(pref + "_gb3", np.asarray(inputs[pref + "_gc_b3"], np.float32).reshape(1, 1))
    K.din(pref + "_ar", np.asarray(inputs[pref + "_adaptive_range"], np.float32).reshape(1, 1))
    return names


def emit_ffn_cp(K, pref):
    P = K.prog
    es = contextlib.ExitStack()
    T = 256
    NSL = SEQ // T
    ph = {}
    ph["cast"] = K.sbuf("cp_cast", [128, T], BF16, es)
    ph["sq"] = K.sbuf("cp_sq", [128, T], BF16, es)
    ph["row"] = K.sbuf("cp_row", [1, 2 * T], F32, es)
    ph["tmp"] = K.sbuf("cp_tmp", [128, T], F32, es)
    out_f, out_f_b = K.sbuf("cp_out", [128, NCH * T], F32, es)
    out2_f, out2_f_b = K.sbuf("cp_out2", [128, NCH * T], F32, es)
    out_bf, out_bf_b = K.sbuf("cp_outb", [128, NCH * T], BF16, es)
    n1_t, n1_b = K.sbuf("cp_n1", [128, NCH * T], BF16, es)
    n2_t, n2_b = K.sbuf("cp_n2", [128, 4 * T], BF16, es)
    n3_t, n3_b = K.sbuf("cp_n3", [128, 2 * T], BF16, es)
    wA, wA_b = K.sbuf("cp_wA", [128, 2048], BF16, es)
    wB, wB_b = K.sbuf("cp_wB", [128, 2048], BF16, es)
    gw_t, gw_b = K.sbuf("cp_gw", [128, (32 + 8 + 2) * 128], BF16, es)
    coupB, coupB_b = K.sbuf("cp_cb", [128, T], F32, es)
    cols, cols_b = K.sbuf("cp_cols", [128, 64], F32, es)
    rows, rows_b = K.sbuf("cp_rows", [1, 4 * T], F32, es)
    sc_t, sc_b = K.sbuf("cp_sc", [1, 8], F32, es)
    psb = [K.psum(f"cp_p{i}", [128, 512], F32, es) for i in range(6)]
    for i in range(4):
        ph[f"ps_ln{i}"] = psb[2 + i]

    K.load(cols[:, 0:8], pref + "_cb1", cols_b)
    K.load(cols[:, 8:12], pref + "_cb2", cols_b)
    K.load(cols[:, 12:14], pref + "_cb3", cols_b)
    K.load(cols[:, 16:20], pref + "_gb1", cols_b)
    K.load(cols[:, 20:22], pref + "_gb2", cols_b)
    K.load(sc_t[0:1, 4:5], pref + "_cb4", sc_b)
    K.load(sc_t[0:1, 5:6], pref + "_gb3", sc_b)
    K.load(sc_t[0:1, 1:2], pref + "_ar", sc_b)
    P.op("vector", lambda g: g.memset(cols[:, 24:25], 1.0), writes=[cols_b])
    P.op("vector", lambda g: g.memset(rows[0:1, 2 * T:3 * T], 1.0), writes=[rows_b])
    P.op("vector", lambda g: g.memset(cols[:, 32:40], 0.0), writes=[cols_b])
    OFF_G1, OFF_G2, OFF_G3 = 0, 32, 40
    K.load(gw_t[:, 0:32 * 128].rearrange("p (mc j) -> p mc j", j=128),
           K.bf[pref + "_gc_w1blk"].rearrange("m c p j -> p (m c) j"), gw_b, K.bfb(pref + "_gc_w1blk"))
    K.load(gw_t[:, 32 * 128:40 * 128].rearrange("p (mc j) -> p mc j", j=128),
           K.bf[pref + "_gc_w2blk"].rearrange("m c p j -> p (m c) j"), gw_b, K.bfb(pref + "_gc_w2blk"))
    K.load(gw_t[:, 40 * 128:42 * 128].rearrange("p (mc j) -> p mc j", j=128),
           K.bf[pref + "_gc_w3blk"][0].rearrange("c p j -> p c j"), gw_b, K.bfb(pref + "_gc_w3blk"))
    P.op("vector", lambda g: g.tensor_single_scalar(sc_t[0:1, 2:3], sc_t[0:1, 1:2], 2.0, ALU.mult),
         rmw=[sc_b])
    P.op("vector", lambda g: g.tensor_scalar(sc_t[0:1, 3:4], sc_t[0:1, 1:2], -1.0, 0.1,
                                             ALU.mult, ALU.add), rmw=[sc_b])

    for sl in range(NSL):
        t0 = sl * T
        xn_views = [out_bf[:, k * T:(k + 1) * T] for k in range(NCH)]
        emit_ln_slice(K, ph, t0, T, xn_views, out_bf_b)
        for k in range(NCH):
            P.op("vector", (lambda k: (lambda g: g.reduce_sum(
                 cols[:, 40 + k:41 + k], out_bf[:, k * T:(k + 1) * T], AX.X)))(k),
                 reads=[out_bf_b], rmw=[cols_b])
            P.op("vector", (lambda k: (lambda g: g.tensor_add(
                 cols[:, 32 + k:33 + k], cols[:, 32 + k:33 + k], cols[:, 40 + k:41 + k])))(k),
                 rmw=[cols_b])
    P.op("vector", lambda g: g.tensor_single_scalar(cols[:, 32:40], cols[:, 32:40],
                                                    1.0 / SEQ, ALU.mult), rmw=[cols_b])
    P.op("vector", lambda g: g.tensor_copy(cols[:, 40:48], cols[:, 32:40]), rmw=[cols_b])
    bank, bank_b = psb[0]
    def mkg1(bank):
        def body(g):
            ins = None
            for m in range(4):
                for k in range(NCH):
                    ins = g.matmul(bank[:, m:m + 1], gw_t[:, (OFF_G1 + m * 8 + k) * 128:(OFF_G1 + m * 8 + k + 1) * 128],
                                   cols[:, 40 + k:41 + k].bitcast(BF16) if False else cols[:, 40 + k:41 + k],
                                   start=(k == 0), stop=(k == NCH - 1))
            return ins
        return body
    gvec, gvec_b = K.sbuf("cp_gvec", [128, 16], BF16, es)
    P.op("vector", lambda g: g.tensor_copy(gvec[:, 0:8], cols[:, 40:48]),
         reads=[cols_b], writes=[gvec_b])
    def mkg(bank, off, nm, nk, src, s0):
        def body(g):
            ins = None
            for m in range(nm):
                for k in range(nk):
                    ins = g.matmul(bank[:, m:m + 1],
                                   gw_t[:, (off + m * nk + k) * 128:(off + m * nk + k + 1) * 128],
                                   src[:, s0 + k:s0 + k + 1],
                                   start=(k == 0), stop=(k == nk - 1))
            return ins
        return body
    P.op("tensor", mkg(bank, OFF_G1, 4, NCH, gvec, 0), reads=[gw_b, gvec_b], writes=[bank_b])
    P.op("vector", lambda g: g.tensor_add(cols[:, 48:52], bank[:, 0:4], cols[:, 16:20]),
         reads=[bank_b, cols_b], rmw=[cols_b])
    P.op("scalar", lambda g: g.activation(gvec[:, 8:12], cols[:, 48:52], AF.Gelu),
         reads=[cols_b], writes=[gvec_b])
    bank2, bank2_b = psb[1]
    P.op("tensor", mkg(bank2, OFF_G2, 2, 4, gvec, 8), reads=[gw_b, gvec_b], writes=[bank2_b])
    P.op("vector", lambda g: g.tensor_add(cols[:, 52:54], bank2[:, 0:2], cols[:, 20:22]),
         reads=[bank2_b, cols_b], rmw=[cols_b])
    P.op("scalar", lambda g: g.activation(gvec[:, 12:14], cols[:, 52:54], AF.Gelu),
         reads=[cols_b], writes=[gvec_b])
    P.op("tensor", mkg(bank, OFF_G3, 1, 2, gvec, 12), reads=[gw_b, gvec_b], writes=[bank_b])
    P.op("vector", lambda g: g.tensor_add(sc_t[0:1, 0:1], bank[0:1, 0:1], sc_t[0:1, 5:6]),
         reads=[bank_b], rmw=[sc_b])
    P.op("scalar", lambda g: g.activation(sc_t[0:1, 0:1], sc_t[0:1, 0:1], AF.Sigmoid), rmw=[sc_b])
    P.op("vector", lambda g: g.tensor_single_scalar(sc_t[0:1, 6:7], sc_t[0:1, 0:1], 0.7, ALU.mult),
         rmw=[sc_b])

    TW = {pref + "_cn_w1blk": (8, 8), pref + "_cn_w2blk": (4, 8),
          pref + "_cn_w3blk": (2, 4), pref + "_cn_w4blk": (1, 2)}
    for sl in range(NSL):
        t0 = sl * T
        xn_views = [out_f[:, k * T:(k + 1) * T] for k in range(NCH)]
        emit_ln_slice(K, ph, t0, T, xn_views, out_f_b)
        for k in range(NCH):
            P.op("vector", (lambda k: (lambda g: g.tensor_copy(
                 out_bf[:, k * T:(k + 1) * T], out_f[:, k * T:(k + 1) * T])))(k),
                 reads=[out_f_b], writes=[out_bf_b])
        cur_f, cur_f_b, alt_f, alt_f_b = out_f, out_f_b, out2_f, out2_f_b
        for cyc in range(3):
            stages = [(pref + "_cn_w1blk", out_bf, out_bf_b, n1_t, n1_b, 0, NCH),
                      (pref + "_cn_w2blk", n1_t, n1_b, n2_t, n2_b, 8, 4),
                      (pref + "_cn_w3blk", n2_t, n2_b, n3_t, n3_b, 12, 2)]
            for wnm, src, src_b, dst, dst_b, boff, nm in stages:
                nmk = TW[wnm]
                for m in range(nm):
                    wbuf, wbuf_b = (wA, wA_b) if m % 2 == 0 else (wB, wB_b)
                    K.load(wbuf[:, 0:nmk[1] * 128].rearrange("p (k j) -> p k j", j=128),
                           K.bf[wnm][m].rearrange("k p j -> p k j"), wbuf_b, K.bfb(wnm))
                    bank, bank_b = psb[m % 2]
                    def mks(bank, wbuf, nk, src, m):
                        def body(g):
                            ins = None
                            for k in range(nk):
                                ins = g.matmul(bank[:, 0:T], wbuf[:, k * 128:(k + 1) * 128],
                                               src[:, k * T:(k + 1) * T],
                                               start=(k == 0), stop=(k == nk - 1))
                            return ins
                        return body
                    P.op("tensor", mks(bank, wbuf, nmk[1], src, m),
                         reads=[wbuf_b, src_b], writes=[bank_b])
                    P.op("scalar", (lambda bank, dst, m, boff: (lambda g: g.activation(
                         dst[:, m * T:(m + 1) * T], bank[:, 0:T], AF.Gelu,
                         bias=cols[:, boff + m:boff + m + 1], scale=cols[:, 24:25])))(bank, dst, m, boff),
                         reads=[bank_b, cols_b], writes=[dst_b])
            wnm = pref + "_cn_w4blk"
            wbuf, wbuf_b = wA, wA_b
            K.load(wbuf[:, 0:2 * 128].rearrange("p (k j) -> p k j", j=128),
                   K.bf[wnm][0].rearrange("k p j -> p k j"), wbuf_b, K.bfb(wnm))
            bank, bank_b = psb[0]
            def mk4(bank, wbuf):
                def body(g):
                    ins = None
                    for k in range(2):
                        ins = g.matmul(bank[:, 0:T], wbuf[:, k * 128:(k + 1) * 128],
                                       n3_t[:, k * T:(k + 1) * T], start=(k == 0), stop=(k == 1))
                    return ins
                return body
            P.op("tensor", mk4(bank, wbuf), reads=[wbuf_b, n3_b], writes=[bank_b])
            P.op("vector", lambda g: g.tensor_scalar_add(rows[0:1, 0:T], bank[0:1, 0:T],
                                                         sc_t[0:1, 4:5]),
                 reads=[bank_b, sc_b], writes=[rows_b])
            P.op("scalar", lambda g: g.activation(rows[0:1, 0:T], rows[0:1, 0:T], AF.Sigmoid),
                 rmw=[rows_b])
            P.op("vector", lambda g: g.scalar_tensor_tensor(
                 rows[0:1, 3 * T:4 * T], rows[0:1, 0:T], 0.3,
                 rows[0:1, 2 * T:3 * T], ALU.mult, ALU.bypass), rmw=[rows_b])
            P.op("vector", lambda g: g.tensor_scalar_add(rows[0:1, 3 * T:4 * T],
                                                         rows[0:1, 3 * T:4 * T], sc_t[0:1, 6:7]),
                 reads=[sc_b], rmw=[rows_b])
            P.op("vector", lambda g: g.tensor_scalar(rows[0:1, T:2 * T], rows[0:1, 3 * T:4 * T],
                                                     sc_t[0:1, 2:3], sc_t[0:1, 3:4],
                                                     ALU.mult, ALU.add),
                 reads=[sc_b], rmw=[rows_b])
            bankB, bankB_b = psb[1]
            P.op("tensor", lambda g: g.matmul(bankB[:, 0:T], K.sb["s_onesrow"][:],
                                              rows[0:1, T:2 * T], start=True, stop=True),
                 reads=[K.bufs["s_onesrow"], rows_b], writes=[bankB_b])
            P.op("vector", lambda g: g.tensor_copy(coupB[:], bankB[:, 0:T]),
                 reads=[bankB_b], writes=[coupB_b])
            TW_MAP = [(0, 4, 1.0), (1, 5, 1.0), (2, 6, -1.0), (3, 7, -1.0),
                      (4, 0, -1.0), (5, 1, -1.0), (6, 2, 1.0), (7, 3, 1.0)]
            for (c, srcc, sgn) in TW_MAP:
                P.op("vector", (lambda c, srcc, cf: (lambda g: g.tensor_mul(
                     ph["tmp"][0][:, 0:T], cf[:, srcc * T:(srcc + 1) * T], coupB[:])))(c, srcc, cur_f),
                     reads=[cur_f_b, coupB_b], writes=[ph["tmp"][1]])
                if sgn > 0:
                    P.op("vector", (lambda c, cf, af: (lambda g: g.tensor_add(
                         af[:, c * T:(c + 1) * T], cf[:, c * T:(c + 1) * T],
                         ph["tmp"][0][:, 0:T])))(c, cur_f, alt_f),
                         reads=[cur_f_b, ph["tmp"][1]], writes=[alt_f_b])
                else:
                    P.op("vector", (lambda c, cf, af: (lambda g: g.tensor_sub(
                         af[:, c * T:(c + 1) * T], cf[:, c * T:(c + 1) * T],
                         ph["tmp"][0][:, 0:T])))(c, cur_f, alt_f),
                         reads=[cur_f_b, ph["tmp"][1]], writes=[alt_f_b])
                if cyc < 2:
                    P.op("vector", (lambda c, af: (lambda g: g.tensor_copy(
                         out_bf[:, c * T:(c + 1) * T], af[:, c * T:(c + 1) * T])))(c, alt_f),
                         reads=[alt_f_b], writes=[out_bf_b])
            cur_f, cur_f_b, alt_f, alt_f_b = alt_f, alt_f_b, cur_f, cur_f_b
        for c in range(NCH):
            P.op("vector", (lambda c, t0, cf: (lambda g: g.scalar_tensor_tensor(
                 K.sb[f"x{c}"][:, t0:t0 + T], cf[:, c * T:(c + 1) * T], 0.5,
                 K.sb[f"x{c}"][:, t0:t0 + T], ALU.mult, ALU.add)))(c, t0, cur_f),
                 reads=[cur_f_b], rmw=[K.bufs[f"x{c}"]])
    P.barrier(K.sb["s_dummy"], K.sb["s_ones_bf"])
    es.close()



NH_MB = 32
CH_MB = 128

def stage_mamba_inputs(K, pref, inputs):
    names = []
    K.din(pref + "_normc", colsify(np.asarray(inputs[pref + "_norm_w"], np.float32)))
    K.din(pref + "_oblkc", colsify(np.asarray(inputs[pref + "_out_b_blk"], np.float32)))
    wblk = np.asarray(inputs[pref + "_out_w_blk"], np.float32)
    K.din(pref + "_wblkf", blockify(wblk[:, 0:1024].T))
    K.din(pref + "_wblkb", blockify(wblk[:, 1024:2048].T))
    names += [pref + "_wblkf", pref + "_wblkb"]
    for dd in ("f", "b"):
        K.din(pref + f"_{dd}inw", blockify(np.asarray(inputs[pref + f"_{dd}_in_w"], np.float32).T,
                                           pad_m_to=35 * 128))
        K.din(pref + f"_{dd}outw", blockify(np.asarray(inputs[pref + f"_{dd}_out_w"], np.float32).T))
        names += [pref + f"_{dd}inw", pref + f"_{dd}outw"]
        cw = np.asarray(inputs[pref + f"_{dd}_conv_w"], np.float32)
        K.din(pref + f"_{dd}convw", np.ascontiguousarray(cw.reshape(18, 128, 4)))
        K.din(pref + f"_{dd}convb", colsify(np.asarray(inputs[pref + f"_{dd}_conv_b"], np.float32)))
        K.din(pref + f"_{dd}dtb", np.asarray(inputs[pref + f"_{dd}_dt_bias"], np.float32).reshape(32, 1))
        K.din(pref + f"_{dd}alog", np.asarray(inputs[pref + f"_{dd}_A_log"], np.float32).reshape(32, 1))
        K.din(pref + f"_{dd}D", np.asarray(inputs[pref + f"_{dd}_D"], np.float32).reshape(1, 32))
        K.din(pref + f"_{dd}gnc", colsify(np.asarray(inputs[pref + f"_{dd}_gnorm_w"], np.float32)))
    if "c_maskpos" not in K.np_inputs:
        p = np.arange(128, dtype=np.float32)
        K.din("c_maskpos", np.where(p[None, :] <= p[:, None], 0.0, 1e30).astype(np.float32))
    return names


def emit_mamba(K, pref):
    P = K.prog
    T = 512
    NSL = SEQ // T
    if DEV_SLICES is not None:
        NSL = DEV_SLICES
    NCC = T // CH_MB
    xnscr = K.dtmp(pref + "!xn", [NCH, 128, SEQ], BF16)
    xnscr_b = K.dram_bufs[pref + "!xn"]
    zscr = K.dtmp(pref + "!z", [16, 128, T], BF16)
    zscr_b = K.dram_bufs[pref + "!z"]
    outer_es = contextlib.ExitStack()
    xn_sh, xn_sh_b = K.sbuf("mb_xnsh", [128, NCH * T], BF16, outer_es)

    es = contextlib.ExitStack()
    ph = {}
    ph["cast"] = K.sbuf("mbA_cast", [128, T], BF16, es)
    ph["sq"] = K.sbuf("mbA_sq", [128, T], BF16, es)
    ph["row"] = K.sbuf("mbA_row", [1, 2 * T], F32, es)
    ph["tmp"] = K.sbuf("mbA_tmp", [128, T], F32, es)
    u_t, u_b = K.sbuf("mbA_u", [128, NCH * T], F32, es)
    xn_t, xn_b = xn_sh, xn_sh_b
    nwc, nwc_b = K.sbuf("mbA_nw", [128, NCH], F32, es)
    psb = [K.psum(f"mbA_p{i}", [128, 512], F32, es) for i in range(6)]
    for i in range(4):
        ph[f"ps_ln{i}"] = psb[i]
    K.load(nwc[:], pref + "_normc", nwc_b)
    for sl in range(NSL):
        t0 = sl * T
        u_views = [u_t[:, k * T:(k + 1) * T] for k in range(NCH)]
        emit_ln_slice(K, ph, t0, T, u_views, u_b)
        ps_s, ps_s_b = psb[4]
        ps_r, ps_r_b = psb[5]
        for k in range(NCH):
            P.op("scalar", (lambda k: (lambda g: g.activation(
                 ph["sq"][0][:, 0:T], u_t[:, k * T:(k + 1) * T], AF.Square)))(k),
                 reads=[u_b], writes=[ph["sq"][1]])
            P.op("tensor", (lambda k: (lambda g: g.matmul(
                 ps_s[0:1, 0:T], K.sb["s_ones_bf"][:, 0:1], ph["sq"][0][:, 0:T],
                 start=(k == 0), stop=(k == NCH - 1))))(k),
                 reads=[K.bufs["s_ones_bf"], ph["sq"][1]], writes=[ps_s_b])
        row_t, row_b = ph["row"]
        P.op("vector", lambda g: g.tensor_scalar(row_t[0:1, 0:T], ps_s[0:1, 0:T],
                                                 1.0 / D, 1e-5, ALU.mult, ALU.add),
             reads=[ps_s_b], writes=[row_b])
        P.op("scalar", lambda g: g.activation(row_t[0:1, 0:T], row_t[0:1, 0:T], AF.Sqrt),
             rmw=[row_b])
        P.op("vector", lambda g: g.reciprocal(row_t[0:1, 0:T], row_t[0:1, 0:T]), rmw=[row_b])
        P.op("tensor", lambda g: g.matmul(ps_r[:, 0:T], K.sb["s_onesrow"][:], row_t[0:1, 0:T],
                                          start=True, stop=True),
             reads=[K.bufs["s_onesrow"], row_b], writes=[ps_r_b])
        for k in range(NCH):
            P.op("vector", (lambda k: (lambda g: g.scalar_tensor_tensor(
                 xn_t[:, k * T:(k + 1) * T], u_t[:, k * T:(k + 1) * T], nwc[:, k:k + 1],
                 ps_r[:, 0:T], ALU.mult, ALU.mult)))(k),
                 reads=[u_b, nwc_b, ps_r_b], writes=[xn_b])
            P.op("vector", (lambda k, t0: (lambda g: g.tensor_add(
                 K.sb[f"x{k}"][:, t0:t0 + T], K.sb[f"x{k}"][:, t0:t0 + T],
                 u_t[:, k * T:(k + 1) * T])))(k, t0),
                 reads=[u_b], rmw=[K.bufs[f"x{k}"]])
            K.store(xnscr[k, :, t0:t0 + T], xn_t[:, k * T:(k + 1) * T], xnscr_b, xn_b)
    P.barrier(K.sb["s_dummy"], K.sb["s_ones_bf"])
    es.close()
    if DEV_MAMBA_PHASE_A_ONLY:
        return

    es = contextlib.ExitStack()
    xn_t, xn_b = xn_sh, xn_sh_b
    td_t = xn_t
    xs_t, xs_b = K.sbuf("mb_xs", [128, 16 * T], BF16, es)
    BC_t, BC_b = K.sbuf("mb_bcx", [128, 2 * T], BF16, es)
    y_t, y_b = K.sbuf("mb_y", [128, 16 * T], BF16, es)
    st_t, st_b = K.sbuf("mb_st", [128, NH_MB * 64], F32, es)
    zc_t, zc_b = K.sbuf("mb_zc", [128, 16 * 128], BF16, es)
    sz_t, sz_b = K.sbuf("mb_sz", [128, 16 * 128], BF16, es)
    wA, wA_b = K.sbuf("mb_wA", [128, 8 * 128], BF16, es)
    wB, wB_b = K.sbuf("mb_wB", [128, 8 * 128], BF16, es)
    dtg, dtg_b = K.sbuf("mb_dtg", [32, 4 * T], F32, es)
    ones32, ones32_b = K.sbuf("mb_ones32", [32, 128], F32, es)
    tmc, tmc_b = K.sbuf("mb_tmc", [128, 64], F32, es)
    dttm, dttm_b = K.sbuf("mb_dttm", [128, 32], F32, es)
    rowb_sb, rowb_sb_b = K.sbuf("mb_rowb", [1, 128], F32, es)
    cols, cols_b = K.sbuf("mb_cols", [128, 117], F32, es)
    carry, carry_b = K.sbuf("mb_carry", [128, 18 * 3], F32, es)
    acc_t, acc_b = K.sbuf("mb_acc", [128, T], F32, es)
    Dt_t, Dt_b = acc_t[:, 0:128], acc_b
    pt_t, pt_b = K.sbuf("mb_pt", [128, 128], BF16, es)
    ptg_t, ptg_b = K.sbuf("mb_ptg", [128, 128], BF16, es)
    cs_t, cs_b = K.sbuf("mb_cs", [128, 128], BF16, es)
    ea_t, ea_b = K.sbuf("mb_ea", [128, 128], BF16, es)
    ea_sc = (ea_t, ea_b)
    stbf_t, stbf_b = K.sbuf("mb_stbf", [128, 64], BF16, es)
    bts_t, bts_b = K.sbuf("mb_bts", [128, 128], BF16, es)
    btm_t, btm_b = K.sbuf("mb_btm", [128, 128], BF16, es)
    xtm_t, xtm_b = K.sbuf("mb_xtm", [128, 128], BF16, es)
    cvw, cvw_b = K.sbuf("mb_cvw", [128, 18 * 4], F32, es)
    rows_m, rows_m_b = K.sbuf("mb_rows", [1, 128], F32, es)
    maskneg_t, maskneg_b = K.sbuf("mb_mn", [128, 128], F32, es)
    maskpos_t, maskpos_b = K.sbuf("mb_mp", [128, 128], F32, es)
    ident_f, ident_f_b = K.sbuf("mb_idf", [128, 128], F32, es)
    drow_t, drow_b = K.sbuf("mb_drow", [1, 32], F32, es)
    psb = [K.psum(f"mb_p{i}", [128, 512], F32, es) for i in range(6)]
    psT, psT_b = K.psum("mb_pT", [128, 1024], BF16, es)
    psTf, psTf_b = K.psum("mb_pTf", [128, 512], F32, es)
    PS_IN = [psb[0], psb[1]]
    PS_RB, PS_RB_b = psb[2]
    PS_G, PS_G_b = psb[3]
    PS_Y, PS_Y_b = psb[4]
    PS_S, PS_S_b = psb[5]

    P.op("vector", lambda g: g.memset(ones32[:], 1.0), writes=[ones32_b])
    P.op("vector", lambda g: g.memset(cols[:, 110:111], 0.0), writes=[cols_b])
    P.op("vector", lambda g: g.memset(cols[:, 111:112], 1.0), writes=[cols_b])
    P.op("vector", lambda g: g.memset(cols[:, 116:117], -1.0), writes=[cols_b])
    K.load(maskneg_t[:], "c_maskneg", maskneg_b)
    K.load(maskpos_t[:], "c_maskpos", maskpos_b)
    K.load(ident_f[:], "c_identf", ident_f_b)

    DT0, DT1, DT2, DT3 = 0, T, 2 * T, 3 * T

    for dd in DEV_MB_DIRS:
        fwd = dd == "f"
        inw = K.bf[pref + f"_{dd}inw"]; inw_b = K.bfb(pref + f"_{dd}inw")
        outw = K.bf[pref + f"_{dd}outw"]; outw_b = K.bfb(pref + f"_{dd}outw")
        wblk = K.bf[pref + ("_wblkf" if fwd else "_wblkb")]
        wblk_b = K.bfb(pref + ("_wblkf" if fwd else "_wblkb"))
        K.load(cols[0:32, 0:1], pref + f"_{dd}alog", cols_b)
        K.load(cols[0:32, 1:2], pref + f"_{dd}dtb", cols_b)
        K.load(cols[:, 66:82], pref + f"_{dd}gnc", cols_b)
        K.load(cols[:, 82:100], pref + f"_{dd}convb", cols_b)
        K.load(cols[:, 100:108], pref + "_oblkc", cols_b)
        K.load(cvw[:].rearrange("p (c k) -> p c k", k=4),
               K.in_aps[pref + f"_{dd}convw"].rearrange("c p k -> p c k"),
               cvw_b, K.dram_bufs[pref + f"_{dd}convw"])
        P.op("scalar", lambda g: g.activation(cols[0:32, 0:1], cols[0:32, 0:1], AF.Exp),
             rmw=[cols_b])
        P.op("vector", lambda g: g.tensor_single_scalar(cols[0:32, 0:1], cols[0:32, 0:1],
                                                        -1.0, ALU.mult), rmw=[cols_b])
        K.load(drow_t[:], pref + f"_{dd}D", drow_b)
        P.op("tensor", lambda g: g.matmul(psTf[:, 0:32], K.sb["s_onesrow"][:], drow_t[:],
                                          start=True, stop=True),
             reads=[K.bufs["s_onesrow"], drow_b], writes=[psTf_b])
        P.op("vector", lambda g: g.tensor_copy(cols[:, 2:34], psTf[:, 0:32]),
             reads=[psTf_b], rmw=[cols_b])
        P.op("vector", lambda g: g.memset(st_t[:], 0.0), writes=[st_b])
        P.op("vector", lambda g: g.memset(carry[:], 0.0), writes=[carry_b])

        sl_order = range(NSL) if fwd else range(NSL - 1, -1, -1)
        for sl in sl_order:
            t0 = sl * T
            for k in range(NCH):
                K.load(xn_t[:, k * T:(k + 1) * T], xnscr[k, :, t0:t0 + T], xn_b, xnscr_b)
            for m in range(35):
                wbuf, wbuf_b = (wA, wA_b) if m % 2 == 0 else (wB, wB_b)
                K.load(wbuf[:].rearrange("p (k j) -> p k j", j=128),
                       inw[m].rearrange("k p j -> p k j"), wbuf_b, inw_b)
                bank, bank_b = PS_IN[m % 2]
                def mkin(bank, wbuf):
                    def body(g):
                        ins = None
                        for k in range(NCH):
                            ins = g.matmul(bank[:], wbuf[:, k * 128:(k + 1) * 128],
                                           xn_t[:, k * T:(k + 1) * T],
                                           start=(k == 0), stop=(k == NCH - 1))
                        return ins
                    return body
                P.op("tensor", mkin(bank, wbuf), reads=[wbuf_b, xn_b], writes=[bank_b])
                if m < 16:
                    P.op("vector", (lambda bank: (lambda g: g.tensor_copy(
                         sz_t[:, 0:T], bank[:])))(bank), reads=[bank_b], writes=[sz_b])
                    K.store(zscr[m], sz_t[:, 0:T], zscr_b, sz_b)
                elif m < 34:
                    cc = m - 16
                    for k4 in range(4):
                        wci = cc * 4 + (k4 if fwd else 3 - k4)
                        if fwd:
                            lo, nmain = 3 - k4, T - 3 + k4
                            if k4 == 0:
                                P.op("vector", (lambda bank, wci, lo, nmain: (lambda g:
                                     g.tensor_single_scalar(acc_t[:, lo:T], bank[:, 0:nmain],
                                                            cvw[:, wci:wci + 1], ALU.mult)))(
                                     bank, wci, lo, nmain),
                                     reads=[bank_b, cvw_b], writes=[acc_b])
                                P.op("vector", (lambda bank, wci, lo: (lambda g:
                                     g.tensor_single_scalar(acc_t[:, 0:lo],
                                                            carry[:, cc * 3:cc * 3 + 3][:, 0 + 0:3],
                                                            cvw[:, wci:wci + 1], ALU.mult)))(
                                     bank, wci, lo) if False else
                                     (lambda bank, wci, lo, cc: (lambda g:
                                      g.tensor_single_scalar(acc_t[:, 0:lo],
                                                             carry[:, cc * 3:cc * 3 + 3],
                                                             cvw[:, wci:wci + 1], ALU.mult)))(
                                      bank, wci, lo, cc),
                                     reads=[carry_b, cvw_b], rmw=[acc_b])
                            else:
                                P.op("vector", (lambda bank, wci, lo, nmain: (lambda g:
                                     g.scalar_tensor_tensor(acc_t[:, lo:T], bank[:, 0:nmain],
                                                            cvw[:, wci:wci + 1], acc_t[:, lo:T],
                                                            ALU.mult, ALU.add)))(bank, wci, lo, nmain),
                                     reads=[bank_b, cvw_b], rmw=[acc_b])
                                if lo > 0:
                                    P.op("vector", (lambda wci, lo, cc, k4: (lambda g:
                                         g.scalar_tensor_tensor(acc_t[:, 0:lo],
                                                                carry[:, cc * 3 + k4:cc * 3 + 3],
                                                                cvw[:, wci:wci + 1], acc_t[:, 0:lo],
                                                                ALU.mult, ALU.add)))(wci, lo, cc, k4),
                                         reads=[carry_b, cvw_b], rmw=[acc_b])
                        else:
                            if k4 == 0:
                                P.op("vector", (lambda bank, wci: (lambda g:
                                     g.tensor_single_scalar(acc_t[:, 0:T], bank[:, 0:T],
                                                            cvw[:, wci:wci + 1], ALU.mult)))(bank, wci),
                                     reads=[bank_b, cvw_b], writes=[acc_b])
                            else:
                                P.op("vector", (lambda bank, wci, k4: (lambda g:
                                     g.scalar_tensor_tensor(acc_t[:, 0:T - k4], bank[:, k4:T],
                                                            cvw[:, wci:wci + 1], acc_t[:, 0:T - k4],
                                                            ALU.mult, ALU.add)))(bank, wci, k4),
                                     reads=[bank_b, cvw_b], rmw=[acc_b])
                                P.op("vector", (lambda wci, cc, k4: (lambda g:
                                     g.scalar_tensor_tensor(acc_t[:, T - k4:T],
                                                            carry[:, cc * 3:cc * 3 + k4],
                                                            cvw[:, wci:wci + 1], acc_t[:, T - k4:T],
                                                            ALU.mult, ALU.add)))(wci, cc, k4),
                                     reads=[carry_b, cvw_b], rmw=[acc_b])
                    if fwd:
                        P.op("vector", (lambda bank, cc: (lambda g: g.tensor_copy(
                             carry[:, cc * 3:cc * 3 + 3], bank[:, T - 3:T])))(bank, cc),
                             reads=[bank_b], rmw=[carry_b])
                    else:
                        P.op("vector", (lambda bank, cc: (lambda g: g.tensor_copy(
                             carry[:, cc * 3:cc * 3 + 3], bank[:, 0:3])))(bank, cc),
                             reads=[bank_b], rmw=[carry_b])
                    if cc < 16:
                        dsts = (xs_t[:, cc * T:(cc + 1) * T], xs_b)
                    else:
                        dsts = (BC_t[:, (cc - 16) * T:(cc - 15) * T], BC_b)
                    P.op("scalar", (lambda dst, cc: (lambda g: g.activation(
                         dst, acc_t[:, 0:T], AF.Silu, bias=cols[:, 82 + cc:83 + cc],
                         scale=cols[:, 111:112])))(dsts[0], cc),
                         reads=[acc_b, cols_b], writes=[dsts[1]])
                else:
                    P.op("vector", (lambda bank: (lambda g: g.tensor_copy(
                         dtg[0:32, DT0:DT0 + T], bank[0:32, :])))(bank),
                         reads=[bank_b], writes=[dtg_b])
            if DEV_MB_DEBUG:
                dbg = K.dout(f"dbg_conv_{dd}_{sl}", [128, 18 * T], BF16)
                for cc2 in range(16):
                    K.store(dbg[:, cc2 * T:(cc2 + 1) * T], xs_t[:, cc2 * T:(cc2 + 1) * T],
                            K.dram_bufs[f"dbg_conv_{dd}_{sl}"], xs_b)
                for cc2 in range(2):
                    K.store(dbg[:, (16 + cc2) * T:(17 + cc2) * T], BC_t[:, cc2 * T:(cc2 + 1) * T],
                            K.dram_bufs[f"dbg_conv_{dd}_{sl}"], BC_b)
            P.op("vector", lambda g: g.tensor_scalar_add(dtg[0:32, DT0:DT0 + T],
                                                         dtg[0:32, DT0:DT0 + T], cols[0:32, 1:2]),
                 reads=[cols_b], rmw=[dtg_b])
            P.op("vector", lambda g: g.tensor_scalar_max(dtg[0:32, DT1:DT1 + T],
                                                         dtg[0:32, DT0:DT0 + T], 0.0),
                 rmw=[dtg_b])
            P.op("scalar", lambda g: g.activation(dtg[0:32, DT3:DT3 + T], dtg[0:32, DT0:DT0 + T],
                                                  AF.Abs), rmw=[dtg_b])
            P.op("vector", lambda g: g.tensor_single_scalar(dtg[0:32, DT3:DT3 + T],
                                                            dtg[0:32, DT3:DT3 + T], -1.0,
                                                            ALU.mult), rmw=[dtg_b])
            P.op("scalar", lambda g: g.activation(dtg[0:32, DT3:DT3 + T], dtg[0:32, DT3:DT3 + T],
                                                  AF.Exp), rmw=[dtg_b])
            P.op("vector", lambda g: g.tensor_single_scalar(dtg[0:32, DT3:DT3 + T],
                                                            dtg[0:32, DT3:DT3 + T], 1.0,
                                                            ALU.add), rmw=[dtg_b])
            P.op("scalar", lambda g: g.activation(dtg[0:32, DT3:DT3 + T], dtg[0:32, DT3:DT3 + T],
                                                  AF.Ln), rmw=[dtg_b])
            P.op("vector", lambda g: g.tensor_add(dtg[0:32, DT1:DT1 + T], dtg[0:32, DT1:DT1 + T],
                                                  dtg[0:32, DT3:DT3 + T]), rmw=[dtg_b])
            P.op("vector", lambda g: g.tensor_scalar_mul(dtg[0:32, DT0:DT0 + T],
                                                         dtg[0:32, DT1:DT1 + T], cols[0:32, 0:1]),
                 reads=[cols_b], rmw=[dtg_b])
            for c in range(NCC):
                P.op("vector", (lambda c: (lambda g: g.tensor_tensor_scan(
                     dtg[0:32, DT2 + c * 128:DT2 + (c + 1) * 128], ones32[:],
                     dtg[0:32, DT0 + c * 128:DT0 + (c + 1) * 128], 0.0, ALU.mult, ALU.add)))(c),
                     reads=[ones32_b], rmw=[dtg_b])
                P.op("vector", (lambda c: (lambda g: g.reduce_sum(
                     cols[0:32, 112 + c:113 + c], dtg[0:32, DT0 + c * 128:DT0 + (c + 1) * 128],
                     AX.X)))(c), reads=[dtg_b], rmw=[cols_b])
            if not fwd:
                P.op("vector", lambda g: g.tensor_sub(dtg[0:32, DT3:DT3 + T],
                                                      dtg[0:32, DT2:DT2 + T],
                                                      dtg[0:32, DT0:DT0 + T]), rmw=[dtg_b])
            if DEV_MB_DEBUG:
                dbg2 = K.dout(f"dbg_dt_{dd}_{sl}", [32, 4 * T], F32)
                K.store(dbg2[:], dtg[0:32, :], K.dram_bufs[f"dbg_dt_{dd}_{sl}"], dtg_b)
            cc_order = range(NCC) if fwd else range(NCC - 1, -1, -1)
            for c in cc_order:
                c0 = c * 128
                P.op("tensor", (lambda c: (lambda g: g.transpose(
                     psTf[:, 0:32], dtg[0:32, DT2 + c * 128:DT2 + (c + 1) * 128],
                     ident_f[0:32, 0:32])))(c),
                     reads=[dtg_b, ident_f_b], writes=[psTf_b])
                P.op("vector", lambda g: g.tensor_copy(tmc[:, 0:32], psTf[:, 0:32]),
                     reads=[psTf_b], writes=[tmc_b])
                src2 = DT3 if not fwd else DT1
                P.op("tensor", (lambda c, src2: (lambda g: g.transpose(
                     psTf[:, 32:64], dtg[0:32, src2 + c * 128:src2 + (c + 1) * 128],
                     ident_f[0:32, 0:32])))(c, src2),
                     reads=[dtg_b, ident_f_b], writes=[psTf_b])
                P.op("vector", lambda g: g.tensor_copy(tmc[:, 32:64], psTf[:, 32:64]),
                     reads=[psTf_b], writes=[tmc_b])
                if not fwd:
                    P.op("tensor", (lambda c: (lambda g: g.transpose(
                         psTf[:, 64:96], dtg[0:32, DT1 + c * 128:DT1 + (c + 1) * 128],
                         ident_f[0:32, 0:32])))(c),
                         reads=[dtg_b, ident_f_b], writes=[psTf_b])
                    P.op("vector", lambda g: g.tensor_copy(dttm[:, 0:32], psTf[:, 64:96]),
                         reads=[psTf_b], writes=[dttm_b])
                P.op("tensor", (lambda c: (lambda g: g.transpose(
                     psTf[0:1, 96:128], cols[0:32, 112 + c:113 + c], ident_f[0:32, 0:32])))(c),
                     reads=[cols_b, ident_f_b], writes=[psTf_b])
                P.op("vector", lambda g: g.tensor_copy(rowb_sb[0:1, 0:32], psTf[0:1, 96:128]),
                     reads=[psTf_b], writes=[rowb_sb_b])
                P.op("tensor", lambda g: g.matmul(psTf[:, 128:160], K.sb["s_onesrow"][:],
                                                  rowb_sb[0:1, 0:32], start=True, stop=True),
                     reads=[K.bufs["s_onesrow"], rowb_sb_b], writes=[psTf_b])
                P.op("vector", lambda g: g.tensor_copy(cols[:, 34:66], psTf[:, 128:160]),
                     reads=[psTf_b], rmw=[cols_b])
                if DEV_MB_DEBUG and c == (0 if fwd else NCC - 1) and sl in (0, NSL - 1):
                    dbgt = K.dout(f"dbgt_{dd}_{sl}_{c}", [128, 256], F32)
                    dbgt_b = K.dram_bufs[f"dbgt_{dd}_{sl}_{c}"]
                    P.op("vector", lambda g: g.tensor_copy(acc_t[0:32, 0:4], cols[0:32, 112:116]),
                         reads=[cols_b], writes=[acc_b])
                    K.store(dbgt[0:32, 0:4], acc_t[0:32, 0:4], dbgt_b, acc_b)
                    P.op("vector", lambda g: g.tensor_copy(acc_t[0:1, 8:40], rowb_sb[0:1, 0:32]),
                         reads=[rowb_sb_b], writes=[acc_b])
                    K.store(dbgt[0:1, 8:40], acc_t[0:1, 8:40], dbgt_b, acc_b)
                    P.op("vector", lambda g: g.tensor_copy(acc_t[:, 64:96], cols[:, 34:66]),
                         reads=[cols_b], writes=[acc_b])
                    K.store(dbgt[:, 64:96], acc_t[:, 64:96], dbgt_b, acc_b)
                P.op("tensor", (lambda c0: (lambda g: g.transpose(
                     psT[:, 0:128], BC_t[:, c0:c0 + 128], K.sb["s_ident_bf"][:])))(c0),
                     reads=[BC_b, K.bufs["s_ident_bf"]], writes=[psT_b])
                P.op("vector", lambda g: g.tensor_copy(btm_t[:], psT[:, 0:128]),
                     reads=[psT_b], writes=[btm_b])
                P.op("tensor", (lambda c0: (lambda g: g.matmul(
                     PS_G[:, 0:128], BC_t[:, c0:c0 + 128], BC_t[:, T + c0:T + c0 + 128],
                     start=True, stop=True)))(c0),
                     reads=[BC_b], writes=[PS_G_b])
                for h in range(NH_MB):
                    hcol = h
                    pref_src = 0 if fwd else 32
                    P.op("tensor", (lambda h, pref_src: (lambda g: g.transpose(
                         psTf[0:1, 160:288], tmc[:, pref_src + h:pref_src + h + 1],
                         ident_f[:])))(h, pref_src),
                         reads=[tmc_b, ident_f_b], writes=[psTf_b])
                    P.op("vector", lambda g: g.tensor_copy(rowb_sb[:], psTf[0:1, 160:288]),
                         reads=[psTf_b], writes=[rowb_sb_b])
                    P.op("tensor", lambda g: g.matmul(PS_RB[:, 0:128], K.sb["s_onesrow"][:],
                                                      rowb_sb[:], start=True, stop=True),
                         reads=[K.bufs["s_onesrow"], rowb_sb_b], writes=[PS_RB_b])
                    if fwd:
                        P.op("vector", (lambda h: (lambda g: g.scalar_tensor_tensor(
                             Dt_t, PS_RB[:, 0:128], tmc[:, h:h + 1], maskneg_t[:],
                             ALU.subtract, ALU.add)))(h),
                             reads=[PS_RB_b, tmc_b, maskneg_b], writes=[Dt_b])
                        P.op("scalar", lambda g: g.activation(pt_t[:], Dt_t, AF.Exp,
                                                              bias=cols[:, 110:111],
                                                              scale=cols[:, 111:112]),
                             reads=[Dt_b, cols_b], writes=[pt_b])
                    else:
                        P.op("vector", (lambda h: (lambda g: g.scalar_tensor_tensor(
                             Dt_t, PS_RB[:, 0:128], tmc[:, 32 + h:32 + h + 1], maskpos_t[:],
                             ALU.subtract, ALU.add)))(h),
                             reads=[PS_RB_b, tmc_b, maskpos_b], writes=[Dt_b])
                        P.op("vector", lambda g: g.tensor_single_scalar(Dt_t, Dt_t,
                                                                        -1.0, ALU.mult),
                             rmw=[Dt_b])
                        P.op("scalar", lambda g: g.activation(pt_t[:], Dt_t, AF.Exp,
                                                              bias=cols[:, 110:111],
                                                              scale=cols[:, 111:112]),
                             reads=[Dt_b, cols_b], writes=[pt_b])
                    P.op("vector", lambda g: g.tensor_mul(ptg_t[:], PS_G[:, 0:128], pt_t[:]),
                         reads=[PS_G_b, pt_b], writes=[ptg_b])
                    if h % 2 == 0:
                        P.op("tensor", (lambda h, c0: (lambda g: g.transpose(
                             psT[:, 128:256], xs_t[:, (h // 2) * T + c0:(h // 2) * T + c0 + 128],
                             K.sb["s_ident_bf"][:])))(h, c0),
                             reads=[xs_b, K.bufs["s_ident_bf"]], writes=[psT_b])
                    xh0 = (h % 2) * 64
                    dtsrc = (tmc, tmc_b, 32) if fwd else (dttm, dttm_b, 0)
                    P.op("vector", (lambda h, xh0, dtsrc: (lambda g: g.tensor_scalar_mul(
                         xtm_t[:, xh0:xh0 + 64], psT[:, 128 + xh0:128 + xh0 + 64],
                         dtsrc[0][:, dtsrc[2] + h:dtsrc[2] + h + 1])))(h, xh0, dtsrc),
                         reads=[psT_b, dtsrc[1]], writes=[xtm_b])
                    P.op("tensor", (lambda xh0: (lambda g: g.matmul(
                         PS_Y[0:64, 0:128], xtm_t[:, xh0:xh0 + 64], ptg_t[:],
                         start=True, stop=False)))(xh0),
                         reads=[xtm_b, ptg_b], writes=[PS_Y_b])
                    if fwd:
                        P.op("scalar", lambda g: g.activation(ea_sc[0][:], PS_RB[:, 0:128],
                                                              AF.Exp, bias=cols[:, 110:111],
                                                              scale=cols[:, 111:112]),
                             reads=[PS_RB_b, cols_b], writes=[ea_sc[1]])
                    else:
                        P.op("scalar", (lambda h: (lambda g: g.activation(
                             ea_sc[0][:], PS_RB[:, 0:128], AF.Exp, bias=cols[:, 34 + h:35 + h],
                             scale=cols[:, 116:117])))(h),
                             reads=[PS_RB_b, cols_b], writes=[ea_sc[1]])
                    P.op("vector", (lambda c0: (lambda g: g.tensor_mul(
                         cs_t[:], BC_t[:, T + c0:T + c0 + 128], ea_sc[0][:])))(c0),
                         reads=[BC_b, ea_sc[1]], writes=[cs_b])
                    P.op("vector", (lambda h: (lambda g: g.tensor_copy(
                         stbf_t[:], st_t[:, h * 64:(h + 1) * 64])))(h),
                         reads=[st_b], writes=[stbf_b])
                    P.op("tensor", lambda g: g.matmul(
                         PS_Y[0:64, 0:128], stbf_t[:], cs_t[:],
                         start=False, stop=True),
                         reads=[stbf_b, cs_b], writes=[PS_Y_b])
                    if fwd:
                        P.op("scalar", (lambda h: (lambda g: g.activation(
                             cols[:, 108:109], tmc[:, h:h + 1], AF.Exp, bias=cols[:, 34 + h:35 + h],
                             scale=cols[:, 116:117])))(h),
                             reads=[tmc_b, cols_b], rmw=[cols_b])
                    else:
                        P.op("scalar", (lambda h: (lambda g: g.activation(
                             cols[:, 108:109], tmc[:, 32 + h:33 + h], AF.Exp,
                             bias=cols[:, 110:111], scale=cols[:, 111:112])))(h),
                             reads=[tmc_b], rmw=[cols_b])
                    P.op("vector", lambda g: g.tensor_scalar_mul(bts_t[:], btm_t[:],
                                                                 cols[:, 108:109]),
                         reads=[btm_b, cols_b], writes=[bts_b])
                    P.op("tensor", (lambda xh0: (lambda g: g.matmul(
                         PS_S[:, 0:64], bts_t[:], xtm_t[:, xh0:xh0 + 64],
                         start=True, stop=True)))(xh0),
                         reads=[bts_b, xtm_b], writes=[PS_S_b])
                    P.op("scalar", (lambda h: (lambda g: g.activation(
                         cols[:, 109:110], cols[:, 34 + h:35 + h], AF.Exp,
                         bias=cols[:, 110:111], scale=cols[:, 111:112])))(h),
                         rmw=[cols_b])
                    P.op("vector", (lambda h: (lambda g: g.scalar_tensor_tensor(
                         st_t[:, h * 64:(h + 1) * 64], st_t[:, h * 64:(h + 1) * 64],
                         cols[:, 109:110], PS_S[:, 0:64], ALU.mult, ALU.add)))(h),
                         reads=[PS_S_b, cols_b], rmw=[st_b])
                    if DEV_MB_DEBUG and h <= 1 and c == (0 if fwd else NCC - 1) and sl in (0, NSL - 1):
                        dbgh = K.dout(f"dbgh_{dd}_{sl}_{c}_{h}", [128, 8 * 128], F32)
                        dbgh_b = K.dram_bufs[f"dbgh_{dd}_{sl}_{c}_{h}"]
                        for nm2, (tt2, bb2) in enumerate([(pt_t, pt_b), (ptg_t, ptg_b),
                                                          (cs_t, cs_b), (bts_t, bts_b)]):
                            P.op("vector", (lambda tt2, nm2: (lambda g: g.tensor_copy(
                                 acc_t[:, 0:128], tt2[:, 0:128])))(tt2, nm2),
                                 reads=[bb2], writes=[acc_b])
                            K.store(dbgh[:, nm2 * 128:(nm2 + 1) * 128], acc_t[:, 0:128],
                                    dbgh_b, acc_b)
                        P.op("vector", (lambda xh0: (lambda g: g.tensor_copy(
                             acc_t[:, 0:64], xtm_t[:, xh0:xh0 + 64])))(xh0),
                             reads=[xtm_b], writes=[acc_b])
                        K.store(dbgh[:, 4 * 128:4 * 128 + 64], acc_t[:, 0:64], dbgh_b, acc_b)
                        P.op("vector", (lambda h: (lambda g: g.tensor_copy(
                             acc_t[:, 0:64], st_t[:, h * 64:(h + 1) * 64])))(h),
                             reads=[st_b], writes=[acc_b])
                        K.store(dbgh[:, 5 * 128:5 * 128 + 64], acc_t[:, 0:64], dbgh_b, acc_b)
                        P.op("vector", lambda g: g.tensor_copy(acc_t[:, 0:32], cols[:, 34:66]),
                             reads=[cols_b], writes=[acc_b])
                        K.store(dbgh[:, 6 * 128:6 * 128 + 32], acc_t[:, 0:32], dbgh_b, acc_b)
                        P.op("vector", (lambda h: (lambda g: g.tensor_copy(
                             acc_t[:, 0:2], cols[:, 108:110])))(h), reads=[cols_b], writes=[acc_b])
                        K.store(dbgh[:, 6 * 128 + 32:6 * 128 + 34], acc_t[:, 0:2], dbgh_b, acc_b)
                    P.op("vector", (lambda h, c0, xh0: (lambda g: g.scalar_tensor_tensor(
                         y_t[:, (h // 2) * T + c0:(h // 2) * T + c0 + 128][xh0:xh0 + 64, :],
                         xs_t[:, (h // 2) * T + c0:(h // 2) * T + c0 + 128][xh0:xh0 + 64, :],
                         cols[xh0:xh0 + 64, 2 + h:3 + h], PS_Y[0:64, 0:128],
                         ALU.mult, ALU.add)))(h, c0, xh0),
                         reads=[xs_b, cols_b, PS_Y_b], writes=[y_b])
                K.load(zc_t[:].rearrange("p (m j) -> p m j", j=128),
                       zscr[:, :, c0:c0 + 128].rearrange("m p j -> p m j"), zc_b, zscr_b)
                P.op("scalar", lambda g: g.activation(sz_t[:, 0:16 * 128], zc_t[:], AF.Silu),
                     reads=[zc_b], writes=[sz_b])
                for sub in range(16):
                    P.op("vector", (lambda sub, c0: (lambda g: g.tensor_mul(
                         y_t[:, sub * T + c0:sub * T + c0 + 128],
                         y_t[:, sub * T + c0:sub * T + c0 + 128],
                         sz_t[:, sub * 128:(sub + 1) * 128])))(sub, c0),
                         rmw=[y_b], reads=[sz_b])
                    P.op("scalar", (lambda sub, c0: (lambda g: g.activation(
                         sz_t[:, sub * 128:(sub + 1) * 128],
                         y_t[:, sub * T + c0:sub * T + c0 + 128], AF.Square)))(sub, c0),
                         reads=[y_b], writes=[sz_b])
                    P.op("tensor", (lambda sub: (lambda g: g.matmul(
                         psTf[0:1, 288:416], K.sb["s_ones_bf"][:, 0:1],
                         sz_t[:, sub * 128:(sub + 1) * 128],
                         start=(sub == 0), stop=(sub == 15))))(sub),
                         reads=[K.bufs["s_ones_bf"], sz_b], writes=[psTf_b])
                P.op("vector", lambda g: g.tensor_scalar(rows_m[0:1, 0:128], psTf[0:1, 288:416],
                                                         1.0 / 2048.0, 1e-5, ALU.mult, ALU.add),
                     reads=[psTf_b], writes=[rows_m_b])
                P.op("scalar", lambda g: g.activation(rows_m[0:1, 0:128], rows_m[0:1, 0:128],
                                                      AF.Sqrt), rmw=[rows_m_b])
                P.op("vector", lambda g: g.reciprocal(rows_m[0:1, 0:128], rows_m[0:1, 0:128]),
                     rmw=[rows_m_b])
                P.op("tensor", lambda g: g.matmul(psTf[:, 288:416], K.sb["s_onesrow"][:],
                                                  rows_m[0:1, 0:128], start=True, stop=True),
                     reads=[K.bufs["s_onesrow"], rows_m_b], writes=[psTf_b])
                for sub in range(16):
                    P.op("vector", (lambda sub, c0: (lambda g: g.scalar_tensor_tensor(
                         y_t[:, sub * T + c0:sub * T + c0 + 128],
                         y_t[:, sub * T + c0:sub * T + c0 + 128],
                         cols[:, 66 + sub:67 + sub], psTf[:, 288:416],
                         ALU.mult, ALU.mult)))(sub, c0),
                         reads=[cols_b, psTf_b], rmw=[y_b])
            if DEV_MB_DEBUG:
                dbg3 = K.dout(f"dbg_y_{dd}_{sl}", [128, 16 * T], BF16)
                for cc2 in range(16):
                    K.store(dbg3[:, cc2 * T:(cc2 + 1) * T], y_t[:, cc2 * T:(cc2 + 1) * T],
                            K.dram_bufs[f"dbg_y_{dd}_{sl}"], y_b)
            for m in range(NCH):
                bank, bank_b = PS_IN[m % 2]
                for half in range(2):
                    wbuf, wbuf_b = (wA, wA_b) if (m * 2 + half) % 2 == 0 else (wB, wB_b)
                    K.load(wbuf[:].rearrange("p (k j) -> p k j", j=128),
                           outw[m, half * 8:(half + 1) * 8].rearrange("k p j -> p k j"),
                           wbuf_b, outw_b)
                    def mko(bank, wbuf, half):
                        def body(g):
                            ins = None
                            for k in range(8):
                                ins = g.matmul(bank[:], wbuf[:, k * 128:(k + 1) * 128],
                                               y_t[:, (half * 8 + k) * T:(half * 8 + k + 1) * T],
                                               start=(half == 0 and k == 0),
                                               stop=(half == 1 and k == 7))
                            return ins
                        return body
                    P.op("tensor", mko(bank, wbuf, half), reads=[wbuf_b, y_b], writes=[bank_b])
                P.op("vector", (lambda bank, m: (lambda g: g.tensor_copy(
                     td_t[:, m * T:(m + 1) * T], bank[:])))(bank, m),
                     reads=[bank_b], writes=[xn_b])
            for m in range(NCH):
                bank, bank_b = PS_IN[m % 2]
                wbuf, wbuf_b = (wA, wA_b) if m % 2 == 0 else (wB, wB_b)
                K.load(wbuf[:].rearrange("p (k j) -> p k j", j=128),
                       wblk[m].rearrange("k p j -> p k j"), wbuf_b, wblk_b)
                def mkb(bank, wbuf):
                    def body(g):
                        ins = None
                        for k in range(8):
                            ins = g.matmul(bank[:], wbuf[:, k * 128:(k + 1) * 128],
                                           td_t[:, k * T:(k + 1) * T],
                                           start=(k == 0), stop=(k == 7))
                        return ins
                    return body
                P.op("tensor", mkb(bank, wbuf), reads=[wbuf_b, xn_b], writes=[bank_b])
                if fwd:
                    P.op("vector", (lambda bank, m, t0: (lambda g: g.scalar_tensor_tensor(
                         K.sb[f"x{m}"][:, t0:t0 + T], bank[:], cols[:, 100 + m:101 + m],
                         K.sb[f"x{m}"][:, t0:t0 + T], ALU.add, ALU.add)))(bank, m, t0),
                         reads=[bank_b, cols_b], rmw=[K.bufs[f"x{m}"]])
                else:
                    P.op("vector", (lambda bank, m, t0: (lambda g: g.tensor_add(
                         K.sb[f"x{m}"][:, t0:t0 + T], K.sb[f"x{m}"][:, t0:t0 + T],
                         bank[:])))(bank, m, t0),
                         reads=[bank_b], rmw=[K.bufs[f"x{m}"]])
    P.barrier(K.sb["s_dummy"], K.sb["s_ones_bf"])
    es.close()
    outer_es.close()
